# version 05
# baseline (speedup 1.0000x reference)
import contextlib

import numpy as np

import concourse.bass as bass
import concourse.mybir as mybir
from concourse.bass_core_rust import IndirectOffsetOnAxis
from concourse.bass_utils import run_bass_kernel_spmd

F32 = mybir.dt.float32
I32 = mybir.dt.int32
OP = mybir.AluOpType
AF = mybir.ActivationFunctionType
AX = mybir.AxisListType

T = 16384
D = 1024
CAP = 2048
DI = 2048
NST = 16
L = 4096
UCH = 256
NCH = L // UCH
BITERS = 40
NEG = -1.0e30
EPS = 1e-5
C2048 = 2048.0
RACE_DETECT = False


class Ctr:
    def __init__(self, h):
        self.h = h
        self.v = 0

    def inc(self, bi, n=1):
        bi.then_inc(self.h, n)
        self.v += n
        return bi

    def wait(self, eng, v=None):
        eng.wait_ge(self.h, self.v if v is None else v)


def build_program():
    es = contextlib.ExitStack()
    nc = bass.Bass("TRN2", target_bir_lowering=False,
                   detect_race_conditions=RACE_DETECT)
    uid = [0]

    def sbuf(name, shape, dtype=F32, stack=None):
        uid[0] += 1
        st_ = stack if stack is not None else es
        return st_.enter_context(
            nc.sbuf_tensor(f"{name}_{uid[0]}", list(shape), dtype))

    def sem(name):
        return Ctr(es.enter_context(nc.semaphore(name)))

    def inp(name, shape, dtype=F32):
        return nc.dram_tensor(name, list(shape), dtype, kind="ExternalInput")

    x_slab = inp("x_slab", [CAP, D])
    x_full = inp("x_full", [T, D])
    centT = inp("centT", [D, 8])
    gfm = inp("gfm", [128, 8])
    bfm = inp("bfm", [128, 8])
    grep = inp("grep", [128, D])
    brep = inp("brep", [128, D])
    w1T = inp("w1T", [D, 4096])
    b1fm = inp("b1fm", [128, 32])
    w2T = inp("w2T", [4096, D])
    b2row = inp("b2row", [1, 1024])
    inwT = inp("inwT", [D, 2 * DI])
    convw = inp("convw", [128, 64])
    convb = inp("convb", [128, 16])
    xprojT_d = inp("xprojT", [DI, 96])
    dtwT_d = inp("dtwT", [64, 2048])
    dtb = inp("dtb", [128, 16])
    dtbrow = inp("dtbrow", [1, 2048])
    a_fm = inp("a_fm", [128, 256])
    dskip = inp("dskip", [128, 16])
    outwT = inp("outwT", [DI, D])
    mgrep = inp("mgrep", [128, D])
    mbrep = inp("mbrep", [128, D])
    ident = inp("ident", [128, 128])
    s_ind = inp("s_ind", [128, 8])
    s_indT = inp("s_indT", [8, 128])
    t16 = inp("t16", [128, 128])
    l8t = inp("l8t", [8, 8])
    onesin = inp("onesin", [128, 128])
    sel16 = inp("sel16", [128, 16])
    bcsel_d = inp("bcsel", [128, 32 * 128])
    idsf = inp("idsf", [128, 16])
    segsel = inp("segsel", [128, 1], I32)
    morder = inp("morder", [128, 32], I32)
    yself = inp("yself", [128, 16], I32)
    yselb = inp("yselb", [128, 16], I32)
    consts = inp("consts", [128, 4])

    out_mine = nc.dram_tensor("out_mine", [CAP, D], F32, kind="ExternalOutput")

    sc_mine = nc.dram_tensor("sc_mine", [8, CAP], F32)
    sc_all = nc.dram_tensor("sc_all", [8, 8, CAP], F32)
    inv_mine = nc.dram_tensor("inv_mine", [T, 1], F32)
    inv_all = nc.dram_tensor("inv_all", [T, 1], F32)
    h_mine = nc.dram_tensor("h_mine", [CAP, D], F32)
    h_all = nc.dram_tensor("h_all", [T, D], F32)
    x1_mine = nc.dram_tensor("x1_mine", [CAP, D], F32)
    x1_all = nc.dram_tensor("x1_all", [T, D], F32)
    x2_mine = nc.dram_tensor("x2_mine", [CAP, D], F32)
    x2_all = nc.dram_tensor("x2_all", [T, D], F32)
    y_mine = nc.dram_tensor("y_mine", [L, D], F32)
    y_pair = nc.dram_tensor("y_pair", [2 * L, D], F32)
    zbuf = nc.dram_tensor("zbuf", [NCH, 128, 16 * UCH], F32)

    MEAN = sbuf("MEAN", [128, 8])
    VAR = sbuf("VAR", [128, 8])
    RSTD = sbuf("RSTD", [128, 8])
    POSF = sbuf("POSF", [128, 16])
    POS32 = sbuf("POS32", [128, 16], I32)
    SEGF = sbuf("SEGF", [128, 16])
    SEG32 = sbuf("SEG32", [128, 16], I32)
    ONE1 = sbuf("ONE1", [128, 1])
    ONER = sbuf("ONER", [1, 128])
    HALFR = sbuf("HALFR", [1, 128])
    IDENT = sbuf("IDENT", [128, 128])
    S_IND = sbuf("S_IND", [128, 8])
    S_INDT = sbuf("S_INDT", [8, 128])
    T16S = sbuf("T16S", [128, 128])
    L8T = sbuf("L8T", [8, 8])
    ONES = sbuf("ONES", [128, 128])
    SEL16 = sbuf("SEL16", [128, 16])
    CENT = sbuf("CENT", [128, 8, 8])
    GFM = sbuf("GFM", [128, 8])
    BFM = sbuf("BFM", [128, 8])
    GRC = sbuf("GRC", [128, D])
    BRC = sbuf("BRC", [128, D])
    B1FM = sbuf("B1FM", [128, 32])
    IDSF = sbuf("IDSF", [128, 16])
    SEGSEL = sbuf("SEGSEL", [128, 1], I32)
    MORDER = sbuf("MORDER", [128, 32], I32)
    YSELF = sbuf("YSELF", [128, 16], I32)
    YSELB = sbuf("YSELB", [128, 16], I32)
    CONVW = sbuf("CONVW", [128, 64])
    CONVB = sbuf("CONVB", [128, 16])
    DTB = sbuf("DTB", [128, 16])
    ONE512 = sbuf("ONE512", [1, 512])
    DSKIP = sbuf("DSKIP", [128, 16])
    AFM = sbuf("AFM", [128, 256])
    ASB = sbuf("ASB", [128, 256])
    CARRY = sbuf("CARRY", [128, 256])
    TAILS = sbuf("TAILS", [128, 16, 3])
    PTm = sbuf("PT", [128, 16])
    VT_ = sbuf("VT", [128, UCH])
    TMP1 = sbuf("TMP1", [128, UCH])
    MEAN2 = sbuf("MEAN2", [128, 2])
    VAR2 = sbuf("VAR2", [128, 2])
    RSTD2 = sbuf("RSTD2", [128, 2])
    CONSTS = sbuf("CONSTS", [128, 4])
    nc.const_aps.register(F32, EPS, CONSTS[:, 0:1])
    nc.const_aps.register(F32, 1.0, CONSTS[:, 1:2])
    nc.const_aps.register(F32, -1.0, CONSTS[:, 2:3])
    nc.const_aps.register(F32, 0.0, CONSTS[:, 3:4])

    def psum(name, shape):
        return es.enter_context(nc.psum_tensor(name, list(shape), F32))

    ps_tr = psum("ps_tr", [128, 128])
    ps_tr2 = psum("ps_tr2", [128, 128])
    ps_a = psum("ps_a", [128, 512])
    ps_b = psum("ps_b", [128, 512])
    ps_c = psum("ps_c", [128, 512])
    ps_sm = psum("ps_sm", [128, 128])
    ps_s2 = psum("ps_s2", [128, 128])
    ps_m = psum("ps_m", [128, 8])

    dS = sem("dS")
    dA = sem("dA")
    dG = sem("dG")
    sc = sem("sc")
    sv = sem("sv")
    st = sem("st")
    ss = sem("ss")
    sg = sem("sg")

    def block_ln(X, NB, out, gb=None, waits=(), scrt=None, stats=None):
        mean, var, rstd = stats if stats else (MEAN, VAR, RSTD)
        sv0, ss0 = sv.v, ss.v
        with nc.Block() as block:
            @block.scalar
            def _(s):
                for c in waits:
                    c.wait(s)
                ss.inc(s.activation(scrt[:, :NB, :], X[:, :NB, :], AF.Square))
                s.wait_ge(sv.h, sv0 + 1)
                ss.inc(s.activation(var[:, :NB], var[:, :NB], AF.Sqrt,
                                    bias=EPS))

            @block.vector
            def _(v):
                for c in waits:
                    c.wait(v)
                v.tensor_reduce(mean[:, :NB], X[:, :NB, :], axis=AX.X,
                                op=OP.add)
                v.wait_ge(ss.h, ss0 + 1)
                v.tensor_reduce(var[:, :NB], scrt[:, :NB, :], axis=AX.X,
                                op=OP.add)
                v.drain()
                v.tensor_scalar(mean[:, :NB], mean[:, :NB], 1.0 / D, None,
                                op0=OP.mult)
                v.tensor_scalar(var[:, :NB], var[:, :NB], 1.0 / D, None,
                                op0=OP.mult)
                v.drain()
                v.tensor_tensor(rstd[:, :NB], mean[:, :NB], mean[:, :NB],
                                op=OP.mult)
                v.drain()
                sv.inc(v.tensor_tensor(var[:, :NB], var[:, :NB],
                                        rstd[:, :NB], op=OP.subtract))
                v.wait_ge(ss.h, ss0 + 2)
                v.reciprocal(rstd[:, :NB], var[:, :NB])
                v.drain()
                for b in range(NB):
                    v.tensor_scalar(out[:, b, :], X[:, b, :],
                                    mean[:, b:b + 1], None, op0=OP.subtract)
                for b in range(NB):
                    bi = v.tensor_scalar(out[:, b, :], out[:, b, :],
                                         rstd[:, b:b + 1], None, op0=OP.mult)
                if gb is not None:
                    g_r, b_r = gb
                    for b in range(NB):
                        v.tensor_tensor(out[:, b, :], out[:, b, :],
                                        g_r[:, :], op=OP.mult)
                    for b in range(NB):
                        bi = v.tensor_tensor(out[:, b, :], out[:, b, :],
                                             b_r[:, :], op=OP.add)
                sv.inc(bi)

    def block_transpose(get_in, N, put_out):
        st0, sv0 = st.v, sv.v
        with nc.Block() as block:
            @block.tensor
            def _(t):
                for n in range(N):
                    pst = ps_tr if n % 2 == 0 else ps_tr2
                    if n >= 2:
                        t.wait_ge(sv.h, sv0 + n - 1)
                    st.inc(t.transpose(pst[:, :], get_in(n), IDENT[:, :]))

            @block.vector
            def _(v):
                for n in range(N):
                    pst = ps_tr if n % 2 == 0 else ps_tr2
                    v.wait_ge(st.h, st0 + n + 1)
                    sv.inc(v.tensor_copy(put_out(n), pst[:, :]))

    def rows_to_fm(X, NB, FMt):
        block_transpose(
            lambda n: X[:, n // 8, 128 * (n % 8):128 * (n % 8 + 1)],
            NB * 8,
            lambda n: FMt[:, n % 8, 128 * (n // 8):128 * (n // 8 + 1)])

    def scores_and_assign(src_dram):
        ph = contextlib.ExitStack()
        with ph:
            XR = sbuf("XR", [128, 8, D], stack=ph)
            FMX = sbuf("FMX", [128, 8, D], stack=ph)
            SCSB = sbuf("SCSB", [8, CAP], stack=ph)
            SCT = sbuf("SCT", [128, 8, 128], stack=ph)
            TIN = sbuf("TIN", [128, 8, 128], stack=ph)
            SM = sbuf("SM", [128, 128], stack=ph)
            TAKF = sbuf("TAKF", [128, 128], stack=ph)
            TAKI = sbuf("TAKI", [128, 128], I32, stack=ph)
            POSM = sbuf("POSM", [128, 128], stack=ph)
            NEGT = sbuf("NEGT", [128, 128], stack=ph)
            GTm = sbuf("GT", [128, 128], stack=ph)
            EQm = sbuf("EQ", [128, 128], stack=ph)
            ICUM = sbuf("ICUM", [128, 128], stack=ph)
            EXC = sbuf("EXC", [128, 128], stack=ph)
            FILL = sbuf("FILL", [128, 128], stack=ph)
            SELm = sbuf("SEL", [128, 128], stack=ph)
            SELF_ = sbuf("SELF", [128, 128], stack=ph)
            CAPM = sbuf("CAPM", [128, 128], stack=ph)
            PVm = sbuf("PV", [128, 128], stack=ph)
            SLBm = sbuf("SLB", [8, 128], stack=ph)
            O3 = sbuf("O3", [16, 128], stack=ph)
            ZT = sbuf("ZT", [32, 128], stack=ph)
            CGT = sbuf("CGT", [128, 8, 8], stack=ph)
            LOHI = sbuf("LOHI", [1, 4], stack=ph)
            SELI = sbuf("SELI", [1, 2], I32, stack=ph)
            MNm = sbuf("MN", [1, 1], stack=ph)
            TOT8 = sbuf("TOT8", [8, 1], stack=ph)
            CNTC = sbuf("CNTC", [128, 1], stack=ph)

            for hf in range(2):
                with nc.Block() as block:
                    @block.sync
                    def _(sync, hf=hf):
                        dS.inc(sync.dma_start(
                            out=XR[:, :, :],
                            in_=bass.AP(src_dram, 1024 * hf * D,
                                        [[D, 128], [128 * D, 8], [1, D]])), 16)
                        dS.wait(sync)

                block_ln(XR, 8, XR, scrt=FMX)
                block_ln(XR, 8, XR, scrt=FMX)
                rows_to_fm(XR, 8, FMX)

                sv0, st0 = sv.v, st.v
                with nc.Block() as block:
                    @block.vector
                    def _(v, hf=hf, sv0=sv0, st0=st0):
                        if hf == 0:
                            for c in range(8):
                                bi = v.tensor_scalar(CGT[:, c, :],
                                                     CENT[:, c, :],
                                                     GFM[:, c:c + 1], None,
                                                     op0=OP.mult)
                            sv.inc(bi)
                        else:
                            sv.inc(v.engine_nop())
                        for tc in range(2):
                            ps = ps_a if tc % 2 == 0 else ps_b
                            v.wait_ge(st.h, st0 + 2 + tc)
                            sv.inc(v.tensor_scalar(
                                SCSB[:8, 1024 * hf + 512 * tc:
                                     1024 * hf + 512 * (tc + 1)],
                                ps[:8, :], ps_m[:8, :1],
                                None, op0=OP.add))

                    @block.tensor
                    def _(t, hf=hf, sv0=sv0, st0=st0):
                        t.wait_ge(sv.h, sv0 + 1)
                        for c in range(8):
                            mm = t.matmul(ps_m[:8, :1], CENT[:, c, :],
                                          BFM[:, c:c + 1], start=(c == 0),
                                          stop=(c == 7))
                        st.inc(mm)
                        for tc in range(2):
                            ps = ps_a if tc % 2 == 0 else ps_b
                            for c in range(8):
                                mm = t.matmul(
                                    ps[:8, :], CGT[:, c, :],
                                    FMX[:, c, 512 * tc:512 * (tc + 1)],
                                    start=(c == 0), stop=(c == 7))
                            st.inc(mm)

            with nc.Block() as block:
                @block.sync
                def _(sync):
                    sv.wait(sync)
                    dS.inc(sync.dma_start(out=sc_mine.ap(),
                                          in_=SCSB[:8, :]), 16)
                    dS.wait(sync)

            with nc.Block() as block:
                @block.gpsimd
                def _(g):
                    sc.inc(g.collective_compute(
                        "AllGather", OP.bypass,
                        replica_groups=[list(range(8))],
                        ins=[sc_mine.ap()], outs=[sc_all.ap()]))
                    sc.wait(g)

            with nc.Block() as block:
                @block.sync
                def _(sync):
                    for e in range(8):
                        dS.inc(sync.dma_start(
                            out=bass.AP(TIN, 128 * e,
                                        [[1024, 128], [16, 8], [1, 16]]),
                            in_=bass.AP(sc_all, 2048 * e,
                                        [[16, 128], [16384, 8], [1, 16]])),
                            16)
                    dS.wait(sync)

                @block.gpsimd
                def _(g):
                    g.memset(TAKF[:, :], 0.0)
                    g.memset(TAKI[:, :], 0)
                    g.memset(POSM[:, :], 0.0)
                    g.memset(NEGT[:, :], NEG)
                    g.memset(ZT[:, :], 0.0)

            block_transpose(
                lambda n: TIN[:, n, :],
                8,
                lambda n: SCT[:, n, :])

            for e in range(8):
                if e < 7:
                    sv0, st0 = sv.v, st.v
                    with nc.Block() as block:
                        @block.vector
                        def _(v, e=e, sv0=sv0, st0=st0):
                            v.tensor_copy(SM[:, :], SCT[:, e, :])
                            v.copy_predicated(SM[:, :], TAKI[:, :],
                                              NEGT[:, :])
                            v.memset(LOHI[:1, 0:1], -1.0e6)
                            v.memset(LOHI[:1, 1:2], 1.0e6)
                            sv.inc(v.memset(LOHI[:1, 2:3], 0.0))
                            for k in range(BITERS):
                                v.wait_ge(st.h, st0 + 2 * k + 1)
                                sv.inc(v.tensor_scalar(
                                    GTm[:, :], SM[:, :], ps_sm[:, :1], None,
                                    op0=OP.is_ge, op1=OP.add,
                                    accum_out=CNTC[:, :1]))
                                v.wait_ge(st.h, st0 + 2 * k + 2)
                                v.tensor_scalar(SELI[:1, 0:1], ps_m[:1, :1],
                                                C2048, None, op0=OP.is_ge)
                                v.tensor_scalar(SELI[:1, 1:2], ps_m[:1, :1],
                                                C2048, None, op0=OP.is_lt)
                                v.tensor_copy(LOHI[:1, 3:4], ps_sm[:1, :1])
                                v.drain()
                                v.copy_predicated(LOHI[:1, 0:1],
                                                  SELI[:1, 0:1],
                                                  LOHI[:1, 3:4])
                                v.copy_predicated(LOHI[:1, 1:2],
                                                  SELI[:1, 1:2],
                                                  LOHI[:1, 3:4])
                                v.drain()
                                sv.inc(v.tensor_tensor(
                                    LOHI[:1, 2:3], LOHI[:1, 0:1],
                                    LOHI[:1, 1:2], op=OP.add))

                        @block.tensor
                        def _(t, e=e, sv0=sv0, st0=st0):
                            for k in range(BITERS):
                                t.wait_ge(sv.h, sv0 + 2 * k + 1)
                                st.inc(t.matmul(ps_sm[:, :1], HALFR[:1, :],
                                                LOHI[:1, 2:3], start=True,
                                                stop=True))
                                t.wait_ge(sv.h, sv0 + 2 * k + 2)
                                st.inc(t.matmul(ps_m[:1, :1], CNTC[:, :1],
                                                ONE1[:, :1], start=True,
                                                stop=True))
                            t.wait_ge(sv.h, sv0 + 2 * BITERS + 1)
                            st.inc(t.matmul(ps_sm[:, :1], ONER[:1, :],
                                            LOHI[:1, 0:1], start=True,
                                            stop=True))
                            st.inc(t.matmul(ps_s2[:, :1], ONER[:1, :],
                                            LOHI[:1, 1:2], start=True,
                                            stop=True))

                def vrank_a(v, M):
                    v.tensor_tensor_scan(ICUM[:, :], ONES[:, :], M[:, :],
                                         0.0, op0=OP.mult, op1=OP.add)
                    sv.inc(v.tensor_tensor(EXC[:, :], ICUM[:, :], M[:, :],
                                           op=OP.subtract))

                def vrank_b(v):
                    v.tensor_copy(SLBm[:8, :], ps_s2[:8, :])
                    sv.inc(v.tensor_copy(TOT8[:8, :1], ps_c[:8, 0:1]))

                def vrank_c(v):
                    sv.inc(v.tensor_scalar(SLBm[:8, :], SLBm[:8, :],
                                           ps_c[:8, 0:1], None, op0=OP.add))

                def trank_a(t):
                    t.matmul(ps_s2[:8, :], S_IND[:, :], EXC[:, :],
                             start=True, stop=True)
                    st.inc(t.matmul(ps_c[:8, 0:1], S_IND[:, :],
                                    ICUM[:, 127:128], start=True, stop=True))

                def trank_b(t):
                    st.inc(t.matmul(ps_c[:8, 0:1], L8T[:8, :8], TOT8[:8, :1],
                                    start=True, stop=True))

                def trank_c(t, M):
                    t.matmul(ps_sm[:, :], S_INDT[:8, :], SLBm[:8, :],
                             start=True, stop=False)
                    st.inc(t.matmul(ps_sm[:, :], T16S[:, :], M[:, :],
                                    start=False, stop=True))

                sv0, st0 = sv.v, st.v
                with nc.Block() as block:
                    @block.vector
                    def _(v, e=e, sv0=sv0, st0=st0):
                        if e < 7:
                            v.tensor_scalar(GTm[:, :], SM[:, :], ps_s2[:, :1],
                                            None, op0=OP.is_ge)
                            v.tensor_scalar(EQm[:, :], SM[:, :], ps_sm[:, :1],
                                            None, op0=OP.is_ge)
                            v.drain()
                            v.tensor_tensor(EQm[:, :], EQm[:, :], GTm[:, :],
                                            op=OP.subtract)
                            sv.inc(v.tensor_reduce(CNTC[:, :1], GTm[:, :],
                                                    axis=AX.X,
                                                    op=OP.add))
                            v.wait_ge(st.h, st0 + 1)
                            v.tensor_scalar(MNm[:1, :1], ps_m[:1, :1], -1.0,
                                            None, op0=OP.mult)
                            sv.inc(v.tensor_scalar(MNm[:1, :1], MNm[:1, :1],
                                                    C2048, None,
                                                    op0=OP.add))
                            vrank_a(v, EQm)
                            v.wait_ge(st.h, st0 + 3)
                            vrank_b(v)
                            v.wait_ge(st.h, st0 + 4)
                            v.drain()
                            vrank_c(v)
                            v.wait_ge(st.h, st0 + 5)
                            v.tensor_scalar(FILL[:, :], ps_sm[:, :],
                                            ps_m[:, 1:2], None, op0=OP.is_lt)
                            v.drain()
                            v.tensor_tensor(FILL[:, :], FILL[:, :], EQm[:, :],
                                            op=OP.mult)
                            v.drain()
                            sv.inc(v.tensor_tensor(SELm[:, :], GTm[:, :],
                                                    FILL[:, :],
                                                    op=OP.add))
                        else:
                            sv.inc(v.tensor_scalar(SELm[:, :], TAKF[:, :],
                                                    0.5, None,
                                                    op0=OP.is_lt))
                        svb = sv.v - sv0
                        stb = 5 if e < 7 else 0
                        vrank_a(v, SELm)
                        v.wait_ge(st.h, st0 + stb + 1)
                        vrank_b(v)
                        v.wait_ge(st.h, st0 + stb + 2)
                        v.drain()
                        vrank_c(v)
                        v.wait_ge(st.h, st0 + stb + 3)
                        v.tensor_scalar(CAPM[:, :], ps_sm[:, :], C2048, None,
                                        op0=OP.is_lt)
                        v.tensor_scalar(PVm[:, :], ps_sm[:, :],
                                        float(CAP * e), None, op0=OP.add)
                        v.drain()
                        v.tensor_tensor(SELF_[:, :], SELm[:, :], CAPM[:, :],
                                        op=OP.mult)
                        v.drain()
                        v.tensor_tensor(PVm[:, :], PVm[:, :], SELF_[:, :],
                                        op=OP.mult)
                        v.tensor_tensor(TAKF[:, :], TAKF[:, :], SELF_[:, :],
                                        op=OP.add)
                        v.drain()
                        v.tensor_tensor(POSM[:, :], POSM[:, :], PVm[:, :],
                                        op=OP.add)
                        sv.inc(v.tensor_scalar(TAKI[:, :], TAKF[:, :], 0.5,
                                                None,
                                                op0=OP.is_ge))

                    @block.tensor
                    def _(t, e=e, sv0=sv0, st0=st0):
                        if e < 7:
                            t.wait_ge(sv.h, sv0 + 1)
                            st.inc(t.matmul(ps_m[:1, :1], CNTC[:, :1],
                                            ONE1[:, :1], start=True,
                                            stop=True))
                            t.wait_ge(sv.h, sv0 + 2)
                            st.inc(t.matmul(ps_m[:, 1:2], ONER[:1, :],
                                            MNm[:1, :1], start=True,
                                            stop=True))
                            t.wait_ge(sv.h, sv0 + 3)
                            trank_a(t)
                            t.wait_ge(sv.h, sv0 + 4)
                            trank_b(t)
                            t.wait_ge(sv.h, sv0 + 5)
                            trank_c(t, EQm)
                            svb2, stb2 = 6, 5
                        else:
                            svb2, stb2 = 1, 0
                        t.wait_ge(sv.h, sv0 + svb2 + 1)
                        trank_a(t)
                        t.wait_ge(sv.h, sv0 + svb2 + 2)
                        trank_b(t)
                        t.wait_ge(sv.h, sv0 + svb2 + 3)
                        trank_c(t, SELm)

            sv0, st0 = sv.v, st.v
            with nc.Block() as block:
                @block.tensor
                def _(t, sv0=sv0, st0=st0):
                    st.inc(t.matmul(ps_s2[:16, :], SEL16[:, :], POSM[:, :],
                                    start=True, stop=True))
                    t.wait_ge(sv.h, sv0 + 1)
                    st.inc(t.transpose(ps_tr[:, :16], O3[:16, :],
                                       IDENT[:16, :16]))

                @block.vector
                def _(v, sv0=sv0, st0=st0):
                    v.wait_ge(st.h, st0 + 1)
                    sv.inc(v.tensor_copy(O3[:16, :], ps_s2[:16, :]))
                    v.wait_ge(st.h, st0 + 2)
                    v.tensor_copy(POSF[:, :], ps_tr[:, :16])
                    v.drain()
                    sv.inc(v.tensor_copy(POS32[:, :], POSF[:, :]))

            with nc.Block() as block:
                @block.sync
                def _(sync):
                    for i2 in range(4):
                        dS.inc(sync.dma_start(
                            out=bass.AP(inv_mine, 4096 * i2,
                                        [[128, 32], [1, 128]]),
                            in_=ZT[:32, :]), 16)
                    dS.wait(sync)

            with nc.Block() as block:
                @block.gpsimd
                def _(g):
                    for b in range(16):
                        dG.inc(g.indirect_dma_start(
                            bass.AP(inv_mine, 0, [[1, T], [1, 1]]),
                            IndirectOffsetOnAxis(POS32[:, b:b + 1], 0),
                            IDSF[:, b:b + 1], None), 16)
                    dG.wait(g)
                    sc.inc(g.collective_compute(
                        "AllReduce", OP.add, replica_groups=[list(range(8))],
                        ins=[inv_mine.ap()], outs=[inv_all.ap()]))
                    sc.wait(g)
                    dG.inc(g.indirect_dma_start(
                        SEGF[:, :], None,
                        bass.AP(inv_all, 0, [[16, 1024], [1, 16]]),
                        IndirectOffsetOnAxis(SEGSEL[:, :1], 0)), 16)
                    dG.wait(g)

            with nc.Block() as block:
                @block.vector
                def _(v):
                    sv.inc(v.tensor_copy(SEG32[:, :], SEGF[:, :]))

    def ffn_compute(gather_dram):
        ph = contextlib.ExitStack()
        with ph:
            XR = sbuf("FXR", [128, 8, D], stack=ph)
            FMX = sbuf("FFM", [128, 8, D], stack=ph)
            G1 = sbuf("G1", [128, 32, 256], stack=ph)
            WB = sbuf("WB", [128, 2, 8, 128], stack=ph)
            WB2 = sbuf("WB2", [128, 2, 32, 128], stack=ph)
            HFM = sbuf("HFM", [128, 8, 256], stack=ph)
            HR = sbuf("HR", [128, 2, D], stack=ph)
            B2ROW = sbuf("B2ROW", [1, 1024], stack=ph)

            with nc.Block() as block:
                @block.sync
                def _(sync):
                    dS.inc(sync.dma_start(out=GRC[:, :], in_=grep.ap()), 16)
                    dS.inc(sync.dma_start(out=BRC[:, :], in_=brep.ap()), 16)
                    dS.inc(sync.dma_start(out=B2ROW[:1, :], in_=b2row.ap()),
                           16)
                    dS.wait(sync)

            for hf in range(2):
                with nc.Block() as block:
                    @block.gpsimd
                    def _(g, hf=hf):
                        for b in range(8):
                            dG.inc(g.indirect_dma_start(
                                XR[:, b, :], None, gather_dram.ap(),
                                IndirectOffsetOnAxis(
                                    SEG32[:, 8 * hf + b:8 * hf + b + 1], 0)),
                                16)
                        dG.wait(g)

                block_ln(XR, 8, XR, scrt=FMX)
                block_ln(XR, 8, XR, gb=(GRC, BRC), scrt=FMX)
                rows_to_fm(XR, 8, FMX)

                for tc in range(4):
                    st0, ss0, sv0, dA0 = st.v, ss.v, sv.v, dA.v
                    with nc.Block() as block:
                        @block.sync
                        def _(sync, hf=hf, tc=tc, st0=st0, sv0=sv0):
                            for F in range(32):
                                if F >= 2:
                                    sync.wait_ge(st.h, st0 + F - 1)
                                dA.inc(sync.dma_start(
                                    out=WB[:, F % 2, :, :],
                                    in_=bass.AP(w1T, 128 * F,
                                                [[4096, 128],
                                                 [128 * 4096, 8],
                                                 [1, 128]])), 16)
                            for dm in range(8):
                                if dm >= 2:
                                    sync.wait_ge(st.h, st0 + 32 + dm - 1)
                                dA.inc(sync.dma_start(
                                    out=WB2[:, dm % 2, :, :],
                                    in_=bass.AP(w2T, 128 * dm,
                                                [[1024, 128],
                                                 [128 * 1024, 32],
                                                 [1, 128]])), 16)
                            sync.wait_ge(sv.h, sv0 + 16)
                            dS.inc(sync.dma_start(
                                out=bass.AP(h_mine, (8 * hf + 2 * tc) * D,
                                            [[16 * D, 128], [D, 2], [1, D]]),
                                in_=HR[:, :, :]), 16)
                            dS.wait(sync)

                        @block.tensor
                        def _(t, hf=hf, tc=tc, st0=st0, ss0=ss0, sv0=sv0,
                              dA0=dA0):
                            for F in range(32):
                                ps = ps_a if F % 2 == 0 else ps_b
                                t.wait_ge(dA.h, dA0 + 16 * (F + 1))
                                if F >= 2:
                                    t.wait_ge(ss.h, ss0 + F - 1)
                                for k in range(8):
                                    mm = t.matmul(
                                        ps[:, :256], WB[:, F % 2, k, :],
                                        FMX[:, k,
                                            256 * tc:256 * (tc + 1)],
                                        start=(k == 0), stop=(k == 7))
                                st.inc(mm)
                            for dm in range(8):
                                ps = ps_a if dm % 2 == 0 else ps_b
                                t.wait_ge(dA.h, dA0 + 512 + 16 * (dm + 1))
                                t.wait_ge(ss.h, ss0 + 32 + max(0, dm - 1))
                                for k in range(32):
                                    t.matmul(ps[:, :256],
                                             WB2[:, dm % 2, k, :],
                                             G1[:, k, :],
                                             start=(k == 0), stop=False)
                                mm = t.matmul(
                                    ps[:, :256],
                                    B2ROW[0:1, 128 * dm:128 * (dm + 1)],
                                    ONE512[:1, :256], start=False,
                                    stop=True)
                                st.inc(mm)
                            for n in range(16):
                                bb, cch = divmod(n, 8)
                                pst = ps_tr if n % 2 == 0 else ps_tr2
                                t.wait_ge(ss.h, ss0 + 33 + cch)
                                if n >= 2:
                                    t.wait_ge(sv.h, sv0 + n - 1)
                                st.inc(t.transpose(
                                    pst[:, :],
                                    HFM[:, cch, 128 * bb:128 * (bb + 1)],
                                    IDENT[:, :]))

                        @block.scalar
                        def _(s, hf=hf, tc=tc, st0=st0):
                            for F in range(32):
                                ps = ps_a if F % 2 == 0 else ps_b
                                s.wait_ge(st.h, st0 + F + 1)
                                ss.inc(s.activation(G1[:, F, :], ps[:, :256],
                                                    AF.Gelu_apprx_tanh,
                                                    bias=B1FM[:, F:F + 1]))
                            for dm in range(8):
                                ps = ps_a if dm % 2 == 0 else ps_b
                                s.wait_ge(st.h, st0 + 32 + dm + 1)
                                ss.inc(s.activation(HFM[:, dm, :],
                                                    ps[:, :256], AF.Copy))

                        @block.vector
                        def _(v, hf=hf, tc=tc, st0=st0):
                            for n in range(16):
                                bb, cch = divmod(n, 8)
                                pst = ps_tr if n % 2 == 0 else ps_tr2
                                v.wait_ge(st.h, st0 + 40 + n + 1)
                                sv.inc(v.tensor_copy(
                                    HR[:, bb, 128 * cch:128 * (cch + 1)],
                                    pst[:, :]))

        with nc.Block() as block:
            @block.gpsimd
            def _(g):
                sc.inc(g.collective_compute(
                    "AllGather", OP.bypass, replica_groups=[list(range(8))],
                    ins=[h_mine.ap()], outs=[h_all.ap()]))
                sc.wait(g)

    def owner_residual(xsrc_dram, dst_mine, dst_all):
        ph = contextlib.ExitStack()
        with ph:
            XR = sbuf("OXR", [128, 8, D], stack=ph)
            HB = sbuf("OHB", [128, 8, D], stack=ph)
            SCX = sbuf("OSCX", [128, 8, D], stack=ph)
            for hf in range(2):
                with nc.Block() as block:
                    @block.sync
                    def _(sync, hf=hf):
                        dS.inc(sync.dma_start(
                            out=XR[:, :, :],
                            in_=bass.AP(xsrc_dram, 8 * hf * D,
                                        [[16 * D, 128], [D, 8], [1, D]])),
                            16)

                    @block.gpsimd
                    def _(g, hf=hf):
                        for b in range(8):
                            dG.inc(g.indirect_dma_start(
                                HB[:, b, :], None, h_all.ap(),
                                IndirectOffsetOnAxis(
                                    POS32[:, 8 * hf + b:8 * hf + b + 1], 0)),
                                16)
                        dG.wait(g)

                block_ln(XR, 8, SCX, waits=(dS,), scrt=SCX)

                sv0 = sv.v
                with nc.Block() as block:
                    @block.vector
                    def _(v, hf=hf):
                        dG.wait(v)
                        v.tensor_tensor(SCX[:, :, :], SCX[:, :, :],
                                        HB[:, :, :], op=OP.add)
                        sv.inc(v.scalar_tensor_tensor(
                            XR[:, :, :], SCX[:, :, :], 0.5, XR[:, :, :],
                            op0=OP.mult, op1=OP.add))

                    @block.sync
                    def _(sync, hf=hf, sv0=sv0):
                        sync.wait_ge(sv.h, sv0 + 1)
                        dS.inc(sync.dma_start(
                            out=bass.AP(dst_mine, 8 * hf * D,
                                        [[16 * D, 128], [D, 8], [1, D]]),
                            in_=XR[:, :, :]), 16)
                        dS.wait(sync)

        if dst_all is not None:
            with nc.Block() as block:
                @block.gpsimd
                def _(g):
                    sc.inc(g.collective_compute(
                        "AllGather", OP.bypass,
                        replica_groups=[list(range(8))],
                        ins=[dst_mine.ap()], outs=[dst_all.ap()]))
                    sc.wait(g)

    def mamba_phase():
        ph = contextlib.ExitStack()
        with ph:
            MXC = sbuf("MXC", [128, 16, UCH], stack=ph)
            MDEL = sbuf("MDEL", [128, 16, UCH], stack=ph)
            MYA = sbuf("MYA", [128, 16, UCH], stack=ph)
            MCALL = sbuf("MCALL", [128, 16, UCH], stack=ph)
            MTRIO = sbuf("MTRIO", [128, 3 * 16 * UCH], stack=ph)
            MREG = sbuf("MREG", [128, 16 * UCH], stack=ph)
            MW = sbuf("MW", [128, 2, 8, 128], stack=ph)
            MWO = sbuf("MWO", [128, 16, 128], stack=ph)
            XPW = sbuf("XPW", [128, 16, 96], stack=ph)
            DTW = sbuf("DTW", [64, 2048], stack=ph)
            BCSEL = sbuf("BCSEL", [128, 32 * 128], stack=ph)
            XIED = sbuf("XIED", [128, UCH + 16], stack=ph)
            CACC = sbuf("CACC", [128, UCH], stack=ph)
            DBL = sbuf("DBL", [96, UCH], stack=ph)
            DUG = sbuf("DUG", [128, UCH], stack=ph)

            MT_W = 3 * 16 * UCH
            MR_W = 16 * UCH

            def trio(off, nb, w):
                if nb == 1:
                    return bass.AP(MTRIO, off, [[MT_W, 128], [1, w]])
                return bass.AP(MTRIO, off, [[MT_W, 128], [w, nb], [1, w]])

            def mreg(off, nb, w):
                if nb == 1:
                    return bass.AP(MREG, off, [[MR_W, 128], [1, w]])
                return bass.AP(MREG, off, [[MR_W, 128], [w, nb], [1, w]])

            DAH_f = trio(0, 1, NST * UCH)
            DBU_f = trio(NST * UCH, 1, NST * UCH)
            HH_f = trio(2 * NST * UCH, 1, NST * UCH)
            HH3 = trio(2 * NST * UCH, NST, UCH)

            def dah_starts():
                return bass.AP(MTRIO, 0, [[MT_W, 128], [UCH, NST]])

            def dbu_starts():
                return bass.AP(MTRIO, NST * UCH, [[MT_W, 128], [UCH, NST]])

            def hh_ends():
                return bass.AP(MTRIO, 2 * NST * UCH + UCH - 1,
                               [[MT_W, 128], [UCH, NST]])

            def hh_perm_nu():
                return bass.AP(MTRIO, 2 * NST * UCH,
                               [[MT_W, 128], [1, UCH], [UCH, NST]])

            XU = mreg(0, 2, D)
            SCRM = mreg(2 * D, 2, D)
            XNF = mreg(2 * D, 8, UCH)
            ZSTG = trio(0, 16, UCH)
            OFM = trio(NST * UCH, 8, UCH)
            YR = trio(2 * NST * UCH, 2, D)

            ss0, sv0, dS0 = ss.v, sv.v, dS.v
            with nc.Block() as block:
                @block.sync
                def _(sync):
                    dS.inc(sync.dma_start(
                        out=XPW[:, :, :],
                        in_=bass.AP(xprojT_d, 0,
                                    [[96, 128], [128 * 96, 16], [1, 96]])),
                        16)
                    dS.inc(sync.dma_start(out=DTW[:64, :], in_=dtwT_d.ap()),
                           16)
                    dS.inc(sync.dma_start(out=BCSEL[:, :], in_=bcsel_d.ap()),
                           16)
                    dS.inc(sync.dma_start(out=GRC[:, :], in_=mgrep.ap()), 16)
                    dS.inc(sync.dma_start(out=BRC[:, :], in_=mbrep.ap()), 16)

                @block.scalar
                def _(s, ss0=ss0, dS0=dS0):
                    s.wait_ge(dS.h, dS0 + 80)
                    ss.inc(s.activation(ASB[:, :], AFM[:, :], AF.Exp))

                @block.vector
                def _(v, ss0=ss0):
                    v.wait_ge(ss.h, ss0 + 1)
                    sv.inc(v.tensor_scalar(ASB[:, :], ASB[:, :], -1.0,
                                            None, op0=OP.mult))

                @block.gpsimd
                def _(g):
                    g.memset(CARRY[:, :], 0.0)
                    sg.inc(g.memset(TAILS[:, :, :], 0.0))

            for cci in range(NCH):
                with nc.Block() as block:
                    @block.gpsimd
                    def _(g, cci=cci):
                        for b in range(2):
                            dG.inc(g.indirect_dma_start(
                                XU[:, b, :], None, x1_all.ap(),
                                IndirectOffsetOnAxis(
                                    MORDER[:, 2 * cci + b:2 * cci + b + 1],
                                    0)), 16)
                        dG.wait(g)

                block_ln(XU, 2, XU, scrt=SCRM, stats=(MEAN2, VAR2, RSTD2))
                block_ln(XU, 2, XU, gb=(GRC, BRC), scrt=SCRM,
                         stats=(MEAN2, VAR2, RSTD2))
                block_transpose(
                    lambda n: XU[:, n // 8, 128 * (n % 8):128 * (n % 8 + 1)],
                    16,
                    lambda n: XNF[:, n % 8,
                                  128 * (n // 8):128 * (n // 8 + 1)])

                st0, ss0, sv0, dA0, dS0 = st.v, ss.v, sv.v, dA.v, dS.v
                with nc.Block() as block:
                    @block.sync
                    def _(sync, cci=cci, st0=st0, ss0=ss0):
                        for m in range(32):
                            if m >= 2:
                                sync.wait_ge(st.h, st0 + m - 1)
                            dA.inc(sync.dma_start(
                                out=MW[:, m % 2, :, :],
                                in_=bass.AP(inwT, 128 * m,
                                            [[2 * DI, 128],
                                             [128 * 2 * DI, 8], [1, 128]])),
                                16)
                        sync.wait_ge(ss.h, ss0 + 32)
                        dS.inc(sync.dma_start(
                            out=bass.AP(zbuf, cci * 128 * 16 * UCH,
                                        [[16 * UCH, 128], [1, 16 * UCH]]),
                            in_=ZSTG), 16)
                        dS.wait(sync)

                    @block.tensor
                    def _(t, cci=cci, st0=st0, ss0=ss0, sv0=sv0, dA0=dA0):
                        for m in range(32):
                            ps = ps_a if m % 2 == 0 else ps_b
                            t.wait_ge(dA.h, dA0 + 16 * (m + 1))
                            if m >= 2:
                                if m - 2 < 16:
                                    t.wait_ge(sv.h, sv0 + (m - 2) + 1)
                                else:
                                    t.wait_ge(ss.h, ss0 + (m - 2) + 1)
                            for k in range(8):
                                mm = t.matmul(ps[:, :UCH], MW[:, m % 2, k, :],
                                              XNF[:, k, :], start=(k == 0),
                                              stop=(k == 7))
                            st.inc(mm)

                    @block.vector
                    def _(v, cci=cci, st0=st0, ss0=ss0):
                        for m in range(16):
                            ps = ps_a if m % 2 == 0 else ps_b
                            v.wait_ge(st.h, st0 + m + 1)
                            if m >= 1:
                                v.wait_ge(ss.h, ss0 + m)
                            v.tensor_copy(XIED[:, 3:3 + UCH], ps[:, :UCH])
                            v.tensor_copy(XIED[:, 0:3], TAILS[:, m, :])
                            v.tensor_copy(TAILS[:, m, :],
                                          XIED[:, UCH:UCH + 3])
                            v.drain()
                            v.tensor_scalar(CACC[:, :], XIED[:, 0:UCH],
                                            CONVW[:, 4 * m:4 * m + 1], None,
                                            op0=OP.mult)
                            for kk in range(1, 4):
                                bi = v.scalar_tensor_tensor(
                                    CACC[:, :], XIED[:, kk:kk + UCH],
                                    CONVW[:, 4 * m + kk:4 * m + kk + 1],
                                    CACC[:, :], op0=OP.mult, op1=OP.add)
                            sv.inc(bi)

                    @block.scalar
                    def _(s, cci=cci, st0=st0, ss0=ss0, sv0=sv0):
                        for m in range(32):
                            ps = ps_a if m % 2 == 0 else ps_b
                            if m < 16:
                                s.wait_ge(sv.h, sv0 + m + 1)
                                ss.inc(s.activation(MXC[:, m, :], CACC[:, :],
                                                    AF.Silu,
                                                    bias=CONVB[:, m:m + 1]))
                            else:
                                s.wait_ge(st.h, st0 + m + 1)
                                ss.inc(s.activation(
                                    trio((m - 16) * UCH, 1, UCH),
                                    ps[:, :UCH], AF.Silu))

                st0, ss0, sv0 = st.v, ss.v, sv.v
                with nc.Block() as block:
                    @block.tensor
                    def _(t, cci=cci, st0=st0, ss0=ss0, sv0=sv0):
                        for g_ in range(16):
                            mm = t.matmul(ps_c[:96, :UCH], XPW[:, g_, :],
                                          MXC[:, g_, :], start=(g_ == 0),
                                          stop=(g_ == 15))
                        st.inc(mm)
                        for g_ in range(16):
                            ps = ps_a if g_ % 2 == 0 else ps_b
                            t.wait_ge(sv.h, sv0 + 1)
                            if g_ >= 2:
                                t.wait_ge(ss.h, ss0 + (g_ - 2) + 1)
                            st.inc(t.matmul(
                                ps[:, :UCH],
                                DTW[:64, 128 * g_:128 * (g_ + 1)],
                                DBL[:64, :], start=True, stop=True))

                    @block.vector
                    def _(v, cci=cci, st0=st0, ss0=ss0):
                        v.wait_ge(st.h, st0 + 1)
                        sv.inc(v.tensor_copy(DBL[:96, :],
                                              ps_c[:96, :UCH]))
                        for g_ in range(16):
                            v.wait_ge(ss.h, ss0 + g_ + 1)
                            sv.inc(v.tensor_tensor(MDEL[:, g_, :],
                                                    VT_[:, :], TMP1[:, :],
                                                    op=OP.add))

                    @block.scalar
                    def _(s, cci=cci, st0=st0, sv0=sv0):
                        for g_ in range(16):
                            s.wait_ge(st.h, st0 + 1 + g_ + 1)
                            if g_ >= 1:
                                s.wait_ge(sv.h, sv0 + 1 + g_)
                            ps = ps_a if g_ % 2 == 0 else ps_b
                            s.activation(VT_[:, :], ps[:, :UCH], AF.Relu,
                                         bias=DTB[:, g_:g_ + 1])
                            s.activation(TMP1[:, :], ps[:, :UCH], AF.Abs,
                                         bias=DTB[:, g_:g_ + 1])
                            s.activation(TMP1[:, :], TMP1[:, :], AF.Exp,
                                         scale=-1.0)
                            ss.inc(s.activation(TMP1[:, :], TMP1[:, :],
                                                AF.Ln, bias=1.0))

                st0, ss0 = st.v, ss.v
                with nc.Block() as block:
                    @block.tensor
                    def _(t, cci=cci, st0=st0, ss0=ss0):
                        for n in range(32):
                            pst = ps_c if n % 2 == 0 else ps_a
                            if n >= 2:
                                t.wait_ge(ss.h, ss0 + n - 1)
                            st.inc(t.matmul(
                                pst[:, :UCH],
                                BCSEL[:96, 128 * n:128 * (n + 1)],
                                DBL[:96, :], start=True, stop=True))

                    @block.scalar
                    def _(s, cci=cci, st0=st0):
                        for n in range(32):
                            pst = ps_c if n % 2 == 0 else ps_a
                            s.wait_ge(st.h, st0 + n + 1)
                            if n < 16:
                                dst = mreg(n * UCH, 1, UCH)
                            else:
                                dst = MCALL[:, n - 16, :]
                            ss.inc(s.activation(dst, pst[:, :UCH], AF.Copy))

                st0, ss0, sv0, sg0 = st.v, ss.v, sv.v, sg.v
                with nc.Block() as block:
                    @block.vector
                    def _(v, cci=cci, ss0=ss0, sv0=sv0, sg0=sg0):
                        ss.wait(v)
                        sv.inc(v.memset(MYA[:, :, :], 0.0))
                        for g_ in range(16):
                            sv.inc(v.tensor_tensor(DUG[:, :],
                                                    MDEL[:, g_, :],
                                                    MXC[:, g_, :],
                                                    op=OP.mult))
                            v.wait_ge(ss.h, ss0 + 16 * (g_ + 1))
                            v.wait_ge(sg.h, sg0 + g_ + 1)
                            v.tensor_tensor(PTm[:, :], dah_starts(),
                                            CARRY[:, 16 * g_:16 * (g_ + 1)],
                                            op=OP.mult)
                            v.drain()
                            v.memset(dah_starts(), 0.0)
                            v.tensor_tensor(dbu_starts(), dbu_starts(),
                                            PTm[:, :], op=OP.add)
                            v.drain()
                            v.tensor_tensor_scan(HH_f, DAH_f, DBU_f, 0.0,
                                                 op0=OP.mult, op1=OP.add)
                            sv.inc(v.tensor_copy(
                                CARRY[:, 16 * g_:16 * (g_ + 1)],
                                hh_ends()))
                            v.tensor_tensor(HH3, HH3, MCALL[:, :, :],
                                            op=OP.mult)
                            v.tensor_reduce(DUG[:, :], hh_perm_nu(),
                                            axis=AX.X, op=OP.add)
                            v.drain()
                            sv.inc(v.tensor_tensor(MYA[:, g_, :],
                                                    MYA[:, g_, :],
                                                    DUG[:, :],
                                                    op=OP.add))

                    @block.scalar
                    def _(s, cci=cci, sv0=sv0):
                        for g_ in range(16):
                            if g_ >= 1:
                                s.wait_ge(sv.h, sv0 + 3 * g_)
                            for n in range(NST):
                                col = 16 * g_ + n
                                ss.inc(s.activation(
                                    trio(n * UCH, 1, UCH),
                                    MDEL[:, g_, :], AF.Exp,
                                    scale=ASB[:, col:col + 1]))

                    @block.gpsimd
                    def _(g, cci=cci, sv0=sv0):
                        for g_ in range(16):
                            g.wait_ge(sv.h, sv0 + 2 + 3 * g_)
                            for n in range(NST):
                                bi = g.tensor_tensor(
                                    trio(NST * UCH + n * UCH, 1, UCH),
                                    DUG[:, :], mreg(n * UCH, 1, UCH),
                                    op=OP.mult)
                            sg.inc(bi)

                st0, ss0, sv0, dA0, dS0 = st.v, ss.v, sv.v, dA.v, dS.v
                with nc.Block() as block:
                    @block.sync
                    def _(sync, cci=cci, st0=st0):
                        dS.inc(sync.dma_start(
                            out=ZSTG,
                            in_=bass.AP(zbuf, cci * 128 * 16 * UCH,
                                        [[16 * UCH, 128], [1, 16 * UCH]])),
                            16)
                        for dm in range(8):
                            if dm >= 1:
                                sync.wait_ge(st.h, st0 + dm)
                            dA.inc(sync.dma_start(
                                out=MWO[:, :, :],
                                in_=bass.AP(outwT, 128 * dm,
                                            [[D, 128], [128 * D, 16],
                                             [1, 128]])), 16)

                    @block.vector
                    def _(v, cci=cci, st0=st0, sv0=sv0, dS0=dS0):
                        v.wait_ge(dS.h, dS0 + 16)
                        for g_ in range(16):
                            v.scalar_tensor_tensor(
                                MYA[:, g_, :], MXC[:, g_, :],
                                DSKIP[:, g_:g_ + 1], MYA[:, g_, :],
                                op0=OP.mult, op1=OP.add)
                            bi = v.tensor_tensor(MYA[:, g_, :],
                                                 MYA[:, g_, :],
                                                 trio(g_ * UCH, 1, UCH),
                                                 op=OP.mult)
                        sv.inc(bi)
                        for dm in range(8):
                            ps = ps_b if dm % 2 == 0 else ps_c
                            v.wait_ge(st.h, st0 + dm + 1)
                            sv.inc(v.tensor_copy(OFM[:, dm, :], ps[:, :UCH]))

                    @block.tensor
                    def _(t, cci=cci, st0=st0, sv0=sv0, dA0=dA0):
                        t.wait_ge(sv.h, sv0 + 1)
                        for dm in range(8):
                            ps = ps_b if dm % 2 == 0 else ps_c
                            t.wait_ge(dA.h, dA0 + 16 * (dm + 1))
                            if dm >= 2:
                                t.wait_ge(sv.h, sv0 + dm)
                            for k in range(16):
                                mm = t.matmul(ps[:, :UCH], MWO[:, k, :],
                                              MYA[:, k, :], start=(k == 0),
                                              stop=(k == 15))
                            st.inc(mm)

                block_transpose(
                    lambda n: OFM[:, n % 8,
                                  128 * (n // 8):128 * (n // 8 + 1)],
                    16,
                    lambda n: bass.AP(
                        MTRIO,
                        2 * NST * UCH + (n // 8) * D + 128 * (n % 8),
                        [[MT_W, 128], [1, 128]]))

                with nc.Block() as block:
                    @block.sync
                    def _(sync, cci=cci):
                        dS.inc(sync.dma_start(
                            out=bass.AP(y_mine, (UCH * cci) * D,
                                        [[D, 128], [128 * D, 2], [1, D]]),
                            in_=YR), 16)
                        dS.wait(sync)

        with nc.Block() as block:
            @block.gpsimd
            def _(g):
                sc.inc(g.collective_compute(
                    "AllGather", OP.bypass,
                    replica_groups=[[0, 1], [2, 3], [4, 5], [6, 7]],
                    ins=[y_mine.ap()], outs=[y_pair.ap()]))
                sc.wait(g)

    def x2_phase():
        ph = contextlib.ExitStack()
        with ph:
            XR = sbuf("X2R", [128, 8, D], stack=ph)
            YB = sbuf("YB", [128, 8, D], stack=ph)
            for hf in range(2):
                with nc.Block() as block:
                    @block.sync
                    def _(sync, hf=hf):
                        dS.inc(sync.dma_start(
                            out=XR[:, :, :],
                            in_=bass.AP(x1_mine, 1024 * hf * D,
                                        [[D, 128], [128 * D, 8], [1, D]])),
                            16)

                    @block.gpsimd
                    def _(g, hf=hf):
                        for b in range(8):
                            dG.inc(g.indirect_dma_start(
                                YB[:, b, :], None, y_pair.ap(),
                                IndirectOffsetOnAxis(
                                    YSELF[:, 8 * hf + b:8 * hf + b + 1], 0)),
                                16)
                        dG.wait(g)

                    @block.vector
                    def _(v, hf=hf):
                        dS.wait(v)
                        dG.wait(v)
                        sv.inc(v.scalar_tensor_tensor(
                            XR[:, :, :], YB[:, :, :], 0.5, XR[:, :, :],
                            op0=OP.mult, op1=OP.add))

                with nc.Block() as block:
                    @block.gpsimd
                    def _(g, hf=hf):
                        for b in range(8):
                            dG.inc(g.indirect_dma_start(
                                YB[:, b, :], None, y_pair.ap(),
                                IndirectOffsetOnAxis(
                                    YSELB[:, 8 * hf + b:8 * hf + b + 1], 0)),
                                16)
                        dG.wait(g)

                    @block.vector
                    def _(v, hf=hf):
                        dG.wait(v)
                        sv.inc(v.scalar_tensor_tensor(
                            XR[:, :, :], YB[:, :, :], 0.5, XR[:, :, :],
                            op0=OP.mult, op1=OP.add))

                sv0 = sv.v
                with nc.Block() as block:
                    @block.sync
                    def _(sync, hf=hf, sv0=sv0):
                        sync.wait_ge(sv.h, sv0)
                        dS.inc(sync.dma_start(
                            out=bass.AP(x2_mine, 1024 * hf * D,
                                        [[D, 128], [128 * D, 8], [1, D]]),
                            in_=XR[:, :, :]), 16)
                        dS.wait(sync)

        with nc.Block() as block:
            @block.gpsimd
            def _(g):
                sc.inc(g.collective_compute(
                    "AllGather", OP.bypass, replica_groups=[list(range(8))],
                    ins=[x2_mine.ap()], outs=[x2_all.ap()]))
                sc.wait(g)

    with nc.Block() as block:
        @block.sync
        def _(sync):
            for src, dst in [
                (ident, IDENT), (s_ind, S_IND), (s_indT, S_INDT),
                (t16, T16S), (l8t, L8T), (onesin, ONES), (sel16, SEL16),
                (gfm, GFM),
                (bfm, BFM), (b1fm, B1FM), (idsf, IDSF),
                (segsel, SEGSEL), (morder, MORDER), (yself, YSELF),
                (yselb, YSELB), (convw, CONVW), (convb, CONVB), (dtb, DTB),
                (dskip, DSKIP), (a_fm, AFM), (consts, CONSTS),
            ]:
                dS.inc(sync.dma_start(out=dst[:, :], in_=src.ap()), 16)
            dS.inc(sync.dma_start(
                out=CENT[:, :, :],
                in_=bass.AP(centT, 0, [[8, 128], [1024, 8], [1, 8]])), 16)
            dS.wait(sync)

        @block.gpsimd
        def _(g):
            g.memset(ONE1[:, :], 1.0)
            g.memset(ONER[:1, :], 1.0)
            g.memset(HALFR[:1, :], 0.5)
            sg.inc(g.memset(ONE512[:1, :], 1.0))

    scores_and_assign(x_slab)
    ffn_compute(x_full)
    owner_residual(x_slab, x1_mine, x1_all)
    mamba_phase()
    x2_phase()
    scores_and_assign(x2_mine)
    ffn_compute(x2_all)
    owner_residual(x2_mine, out_mine, None)

    es.close()
    return nc


_CACHED = {}


def kernel(**inputs):
    xf = np.ascontiguousarray(
        np.asarray(inputs["L0_attn_x"], np.float32).reshape(T, D))
    cent = np.asarray(inputs["L0_ffn_centroids"], np.float32)
    ffng = np.asarray(inputs["L0_ffn_ln_g"], np.float32)
    ffnb = np.asarray(inputs["L0_ffn_ln_b"], np.float32)
    w1 = np.asarray(inputs["L0_ffn_w1"], np.float32)
    b1 = np.asarray(inputs["L0_ffn_b1"], np.float32)
    w2 = np.asarray(inputs["L0_ffn_w2"], np.float32)
    b2 = np.asarray(inputs["L0_ffn_b2"], np.float32)
    mgv = np.asarray(inputs["L0_attn_ln_g"], np.float32)
    mbv = np.asarray(inputs["L0_attn_ln_b"], np.float32)

    def fm(vec, nb):
        return np.ascontiguousarray(vec.reshape(nb, 128).T)

    if "nc" not in _CACHED:
        _CACHED["nc"] = build_program()
    nc = _CACHED["nc"]

    ident = np.eye(128, dtype=np.float32)
    s_ind = np.zeros((128, 8), np.float32)
    s_ind[np.arange(128), np.arange(128) // 16] = 1.0
    s_indT = np.ascontiguousarray(s_ind.T)
    t16 = np.zeros((128, 128), np.float32)
    for pp in range(128):
        for p in range(128):
            if pp // 16 == p // 16 and pp % 16 < p % 16:
                t16[pp, p] = 1.0
    onesin = np.ones((128, 128), np.float32)
    l8t_np = np.zeros((8, 8), np.float32)
    for r in range(8):
        for c in range(8):
            if r < c:
                l8t_np[r, c] = 1.0
    bcsel_np = np.zeros((128, 32 * 128), np.float32)
    for n in range(16):
        bcsel_np[64 + n, 128 * n:128 * (n + 1)] = 1.0
        bcsel_np[80 + n, 128 * (16 + n):128 * (17 + n)] = 1.0

    jj = np.arange(CAP)
    in_maps = []
    for i in range(8):
        bb_ = i >> 1
        dd = i & 1
        pre = "b_" if dd else "f_"
        inw = np.asarray(inputs[f"L0_attn_{pre}in_w"], np.float32)
        cw = np.asarray(inputs[f"L0_attn_{pre}conv_w"], np.float32)
        cb = np.asarray(inputs[f"L0_attn_{pre}conv_b"], np.float32)
        xpw = np.asarray(inputs[f"L0_attn_{pre}xproj_w"], np.float32)
        dtw = np.asarray(inputs[f"L0_attn_{pre}dt_w"], np.float32)
        dtb_ = np.asarray(inputs[f"L0_attn_{pre}dt_b"], np.float32)
        alog = np.asarray(inputs[f"L0_attn_{pre}A_log"], np.float32)
        dsk = np.asarray(inputs[f"L0_attn_{pre}D_skip"], np.float32)
        outw = np.asarray(inputs[f"L0_attn_{pre}out_w"], np.float32)

        sel16_np = np.zeros((128, 16), np.float32)
        for c in range(16):
            sel16_np[16 * i + c, c] = 1.0
        idsf_np = np.zeros((128, 16), np.float32)
        idsf_np[jj // 16, jj % 16] = (2048 * i + jj).astype(np.float32)
        segsel_np = (128 * i + np.arange(128, dtype=np.int32)).reshape(128, 1)
        morder_np = np.zeros((128, 32), np.int32)
        u = np.arange(L)
        rows = 4096 * bb_ + (4095 - u if dd else u)
        morder_np[u % 128, u // 128] = rows.astype(np.int32)
        ysf = np.zeros((128, 16), np.int32)
        ysb = np.zeros((128, 16), np.int32)
        tb = (i % 2) * 2048 + jj
        ysf[jj % 128, jj // 128] = tb.astype(np.int32)
        ysb[jj % 128, jj // 128] = (L + (4095 - tb)).astype(np.int32)

        convw_np = np.zeros((128, 64), np.float32)
        ch = np.arange(DI)
        for kk in range(4):
            convw_np[ch % 128, (ch // 128) * 4 + kk] = cw[:, kk]
        a_np = np.zeros((128, 256), np.float32)
        for n in range(16):
            a_np[ch % 128, (ch // 128) * 16 + n] = alog[:, n]

        in_maps.append({
            "x_slab": np.ascontiguousarray(xf[2048 * i:2048 * (i + 1)]),
            "x_full": xf,
            "centT": np.ascontiguousarray(cent.T),
            "gfm": fm(ffng, 8), "bfm": fm(ffnb, 8),
            "grep": np.ascontiguousarray(np.tile(ffng, (128, 1))),
            "brep": np.ascontiguousarray(np.tile(ffnb, (128, 1))),
            "w1T": np.ascontiguousarray(w1[i]),
            "b1fm": fm(b1[i], 32),
            "w2T": np.ascontiguousarray(w2[i]),
            "b2row": np.ascontiguousarray(b2[i].reshape(1, 1024)),
            "inwT": np.ascontiguousarray(inw.T),
            "convw": convw_np,
            "convb": fm(cb, 16),
            "xprojT": np.ascontiguousarray(xpw.T),
            "dtwT": np.ascontiguousarray(dtw.T),
            "dtb": fm(dtb_, 16),
            "dtbrow": np.ascontiguousarray(dtb_.reshape(1, 2048)),
            "a_fm": a_np,
            "dskip": fm(dsk, 16),
            "outwT": np.ascontiguousarray(outw.T),
            "mgrep": np.ascontiguousarray(np.tile(mgv, (128, 1))),
            "mbrep": np.ascontiguousarray(np.tile(mbv, (128, 1))),
            "ident": ident, "s_ind": s_ind, "s_indT": s_indT, "t16": t16,
            "onesin": onesin, "sel16": sel16_np, "bcsel": bcsel_np,
            "l8t": l8t_np,
            "idsf": idsf_np, "segsel": segsel_np, "morder": morder_np,
            "yself": ysf, "yselb": ysb,
            "consts": np.tile(
                np.array([[EPS, 1.0, -1.0, 0.0]], np.float32), (128, 1)),
        })

    res = run_bass_kernel_spmd(nc, in_maps, core_ids=list(range(8)))
    out = np.concatenate([np.asarray(r["out_mine"]) for r in res.results],
                         axis=0)
    return out.reshape(4, 4096, 1024)


# version 06
# speedup vs baseline: 1.0272x; 1.0272x over previous
import contextlib

import numpy as np

import concourse.bass as bass
import concourse.mybir as mybir
from concourse.bass_core_rust import IndirectOffsetOnAxis
from concourse.bass_utils import run_bass_kernel_spmd

F32 = mybir.dt.float32
I32 = mybir.dt.int32
OP = mybir.AluOpType
AF = mybir.ActivationFunctionType
AX = mybir.AxisListType

T = 16384
D = 1024
CAP = 2048
DI = 2048
NST = 16
L = 4096
UCH = 256
NCH = L // UCH
BITERS = 40
NEG = -1.0e30
EPS = 1e-5
C2048 = 2048.0
RACE_DETECT = False


class Ctr:
    def __init__(self, h):
        self.h = h
        self.v = 0

    def inc(self, bi, n=1):
        bi.then_inc(self.h, n)
        self.v += n
        return bi

    def wait(self, eng, v=None):
        eng.wait_ge(self.h, self.v if v is None else v)


def build_program():
    es = contextlib.ExitStack()
    nc = bass.Bass("TRN2", target_bir_lowering=False,
                   detect_race_conditions=RACE_DETECT)
    uid = [0]

    def sbuf(name, shape, dtype=F32, stack=None):
        uid[0] += 1
        st_ = stack if stack is not None else es
        return st_.enter_context(
            nc.sbuf_tensor(f"{name}_{uid[0]}", list(shape), dtype))

    def sem(name):
        return Ctr(es.enter_context(nc.semaphore(name)))

    def inp(name, shape, dtype=F32):
        return nc.dram_tensor(name, list(shape), dtype, kind="ExternalInput")

    x_slab = inp("x_slab", [CAP, D])
    x_full = inp("x_full", [T, D])
    centT = inp("centT", [D, 8])
    gfm = inp("gfm", [128, 8])
    bfm = inp("bfm", [128, 8])
    grep = inp("grep", [128, D])
    brep = inp("brep", [128, D])
    w1T = inp("w1T", [D, 4096])
    b1fm = inp("b1fm", [128, 32])
    w2T = inp("w2T", [4096, D])
    b2row = inp("b2row", [1, 1024])
    inwT = inp("inwT", [D, 2 * DI])
    convw = inp("convw", [128, 64])
    convb = inp("convb", [128, 16])
    xprojT_d = inp("xprojT", [DI, 96])
    dtwT_d = inp("dtwT", [64, 2048])
    dtb = inp("dtb", [128, 16])
    dtbrow = inp("dtbrow", [1, 2048])
    a_fm = inp("a_fm", [128, 256])
    dskip = inp("dskip", [128, 16])
    outwT = inp("outwT", [DI, D])
    mgrep = inp("mgrep", [128, D])
    mbrep = inp("mbrep", [128, D])
    ident = inp("ident", [128, 128])
    s_ind = inp("s_ind", [128, 8])
    s_indT = inp("s_indT", [8, 128])
    t16 = inp("t16", [128, 128])
    l8t = inp("l8t", [8, 8])
    onesin = inp("onesin", [128, 128])
    sel16 = inp("sel16", [128, 16])
    bcsel_d = inp("bcsel", [128, 32 * 128])
    idsf = inp("idsf", [128, 16])
    segsel = inp("segsel", [128, 1], I32)
    morder = inp("morder", [128, 32], I32)
    yself = inp("yself", [128, 16], I32)
    yselb = inp("yselb", [128, 16], I32)
    consts = inp("consts", [128, 4])

    out_mine = nc.dram_tensor("out_mine", [CAP, D], F32, kind="ExternalOutput")

    sc_mine = nc.dram_tensor("sc_mine", [8, CAP], F32)
    sc_all = nc.dram_tensor("sc_all", [8, 8, CAP], F32)
    inv_mine = nc.dram_tensor("inv_mine", [T, 1], F32)
    inv_all = nc.dram_tensor("inv_all", [T, 1], F32)
    h_mine = nc.dram_tensor("h_mine", [CAP, D], F32)
    h_all = nc.dram_tensor("h_all", [T, D], F32)
    x1_mine = nc.dram_tensor("x1_mine", [CAP, D], F32)
    x1_all = nc.dram_tensor("x1_all", [T, D], F32)
    x2_mine = nc.dram_tensor("x2_mine", [CAP, D], F32)
    x2_all = nc.dram_tensor("x2_all", [T, D], F32)
    y_mine = nc.dram_tensor("y_mine", [L, D], F32)
    y_pair = nc.dram_tensor("y_pair", [2 * L, D], F32)
    zbuf = nc.dram_tensor("zbuf", [NCH, 128, 16 * UCH], F32)

    MEAN = sbuf("MEAN", [128, 8])
    VAR = sbuf("VAR", [128, 8])
    RSTD = sbuf("RSTD", [128, 8])
    POSF = sbuf("POSF", [128, 16])
    POS32 = sbuf("POS32", [128, 16], I32)
    SEGF = sbuf("SEGF", [128, 16])
    SEG32 = sbuf("SEG32", [128, 16], I32)
    ONE1 = sbuf("ONE1", [128, 1])
    ONER = sbuf("ONER", [1, 128])
    HALFR = sbuf("HALFR", [1, 128])
    IDENT = sbuf("IDENT", [128, 128])
    S_IND = sbuf("S_IND", [128, 8])
    S_INDT = sbuf("S_INDT", [8, 128])
    T16S = sbuf("T16S", [128, 128])
    L8T = sbuf("L8T", [8, 8])
    ONES = sbuf("ONES", [128, 128])
    SEL16 = sbuf("SEL16", [128, 16])
    CENT = sbuf("CENT", [128, 8, 8])
    GFM = sbuf("GFM", [128, 8])
    BFM = sbuf("BFM", [128, 8])
    GRC = sbuf("GRC", [128, D])
    BRC = sbuf("BRC", [128, D])
    B1FM = sbuf("B1FM", [128, 32])
    IDSF = sbuf("IDSF", [128, 16])
    SEGSEL = sbuf("SEGSEL", [128, 1], I32)
    MORDER = sbuf("MORDER", [128, 32], I32)
    YSELF = sbuf("YSELF", [128, 16], I32)
    YSELB = sbuf("YSELB", [128, 16], I32)
    CONVW = sbuf("CONVW", [128, 64])
    CONVB = sbuf("CONVB", [128, 16])
    DTB = sbuf("DTB", [128, 16])
    ONE512 = sbuf("ONE512", [1, 512])
    DSKIP = sbuf("DSKIP", [128, 16])
    AFM = sbuf("AFM", [128, 256])
    ASB = sbuf("ASB", [128, 256])
    CARRY = sbuf("CARRY", [128, 256])
    TAILS = sbuf("TAILS", [128, 16, 3])
    PTm = sbuf("PT", [128, 16])
    VT_ = sbuf("VT", [128, UCH])
    TMP1 = sbuf("TMP1", [128, UCH])
    MEAN2 = sbuf("MEAN2", [128, 2])
    VAR2 = sbuf("VAR2", [128, 2])
    RSTD2 = sbuf("RSTD2", [128, 2])
    CONSTS = sbuf("CONSTS", [128, 4])
    nc.const_aps.register(F32, EPS, CONSTS[:, 0:1])
    nc.const_aps.register(F32, 1.0, CONSTS[:, 1:2])
    nc.const_aps.register(F32, -1.0, CONSTS[:, 2:3])
    nc.const_aps.register(F32, 0.0, CONSTS[:, 3:4])

    def psum(name, shape):
        return es.enter_context(nc.psum_tensor(name, list(shape), F32))

    ps_tr = psum("ps_tr", [128, 128])
    ps_tr2 = psum("ps_tr2", [128, 128])
    ps_a = psum("ps_a", [128, 512])
    ps_b = psum("ps_b", [128, 512])
    ps_c = psum("ps_c", [128, 512])
    ps_sm = psum("ps_sm", [128, 128])
    ps_s2 = psum("ps_s2", [128, 128])
    ps_m = psum("ps_m", [128, 8])

    dS = sem("dS")
    dA = sem("dA")
    dG = sem("dG")
    sc = sem("sc")
    sv = sem("sv")
    st = sem("st")
    ss = sem("ss")
    sg = sem("sg")

    def block_ln(X, NB, out, gb=None, waits=(), scrt=None, stats=None):
        mean, var, rstd = stats if stats else (MEAN, VAR, RSTD)
        sv0, ss0 = sv.v, ss.v
        with nc.Block() as block:
            @block.scalar
            def _(s):
                for c in waits:
                    c.wait(s)
                ss.inc(s.activation(scrt[:, :NB, :], X[:, :NB, :], AF.Square))
                s.wait_ge(sv.h, sv0 + 1)
                ss.inc(s.activation(var[:, :NB], var[:, :NB], AF.Sqrt,
                                    bias=EPS))

            @block.vector
            def _(v):
                for c in waits:
                    c.wait(v)
                v.tensor_reduce(mean[:, :NB], X[:, :NB, :], axis=AX.X,
                                op=OP.add)
                v.wait_ge(ss.h, ss0 + 1)
                v.tensor_reduce(var[:, :NB], scrt[:, :NB, :], axis=AX.X,
                                op=OP.add)
                v.drain()
                v.tensor_scalar(mean[:, :NB], mean[:, :NB], 1.0 / D, None,
                                op0=OP.mult)
                v.tensor_scalar(var[:, :NB], var[:, :NB], 1.0 / D, None,
                                op0=OP.mult)
                v.drain()
                v.tensor_tensor(rstd[:, :NB], mean[:, :NB], mean[:, :NB],
                                op=OP.mult)
                v.drain()
                sv.inc(v.tensor_tensor(var[:, :NB], var[:, :NB],
                                        rstd[:, :NB], op=OP.subtract))
                v.wait_ge(ss.h, ss0 + 2)
                v.reciprocal(rstd[:, :NB], var[:, :NB])
                v.drain()
                for b in range(NB):
                    v.tensor_scalar(out[:, b, :], X[:, b, :],
                                    mean[:, b:b + 1], None, op0=OP.subtract)
                for b in range(NB):
                    bi = v.tensor_scalar(out[:, b, :], out[:, b, :],
                                         rstd[:, b:b + 1], None, op0=OP.mult)
                if gb is not None:
                    g_r, b_r = gb
                    for b in range(NB):
                        v.tensor_tensor(out[:, b, :], out[:, b, :],
                                        g_r[:, :], op=OP.mult)
                    for b in range(NB):
                        bi = v.tensor_tensor(out[:, b, :], out[:, b, :],
                                             b_r[:, :], op=OP.add)
                sv.inc(bi)

    def block_transpose(get_in, N, put_out):
        st0, sv0 = st.v, sv.v
        with nc.Block() as block:
            @block.tensor
            def _(t):
                for n in range(N):
                    pst = ps_tr if n % 2 == 0 else ps_tr2
                    if n >= 2:
                        t.wait_ge(sv.h, sv0 + n - 1)
                    st.inc(t.transpose(pst[:, :], get_in(n), IDENT[:, :]))

            @block.vector
            def _(v):
                for n in range(N):
                    pst = ps_tr if n % 2 == 0 else ps_tr2
                    v.wait_ge(st.h, st0 + n + 1)
                    sv.inc(v.tensor_copy(put_out(n), pst[:, :]))

    def rows_to_fm(X, NB, FMt):
        block_transpose(
            lambda n: X[:, n // 8, 128 * (n % 8):128 * (n % 8 + 1)],
            NB * 8,
            lambda n: FMt[:, n % 8, 128 * (n // 8):128 * (n // 8 + 1)])

    def scores_and_assign(src_dram):
        ph = contextlib.ExitStack()
        with ph:
            XR = sbuf("XR", [128, 8, D], stack=ph)
            FMX = sbuf("FMX", [128, 8, D], stack=ph)
            SCSB = sbuf("SCSB", [8, CAP], stack=ph)
            SCT = sbuf("SCT", [128, 8, 128], stack=ph)
            TIN = sbuf("TIN", [128, 8, 128], stack=ph)
            SM = sbuf("SM", [128, 128], stack=ph)
            TAKF = sbuf("TAKF", [128, 128], stack=ph)
            TAKI = sbuf("TAKI", [128, 128], I32, stack=ph)
            POSM = sbuf("POSM", [128, 128], stack=ph)
            NEGT = sbuf("NEGT", [128, 128], stack=ph)
            GTm = sbuf("GT", [128, 128], stack=ph)
            EQm = sbuf("EQ", [128, 128], stack=ph)
            ICUM = sbuf("ICUM", [128, 128], stack=ph)
            EXC = sbuf("EXC", [128, 128], stack=ph)
            FILL = sbuf("FILL", [128, 128], stack=ph)
            SELm = sbuf("SEL", [128, 128], stack=ph)
            SELF_ = sbuf("SELF", [128, 128], stack=ph)
            CAPM = sbuf("CAPM", [128, 128], stack=ph)
            PVm = sbuf("PV", [128, 128], stack=ph)
            SLBm = sbuf("SLB", [8, 128], stack=ph)
            O3 = sbuf("O3", [16, 128], stack=ph)
            ZT = sbuf("ZT", [32, 128], stack=ph)
            CGT = sbuf("CGT", [128, 8, 8], stack=ph)
            LOHI = sbuf("LOHI", [1, 4], stack=ph)
            SELI = sbuf("SELI", [1, 2], I32, stack=ph)
            MNm = sbuf("MN", [1, 1], stack=ph)
            TOT8 = sbuf("TOT8", [8, 1], stack=ph)
            CNTC = sbuf("CNTC", [128, 1], stack=ph)

            for hf in range(2):
                with nc.Block() as block:
                    @block.sync
                    def _(sync, hf=hf):
                        dS.inc(sync.dma_start(
                            out=XR[:, :, :],
                            in_=bass.AP(src_dram, 1024 * hf * D,
                                        [[D, 128], [128 * D, 8], [1, D]])), 16)
                        dS.wait(sync)

                block_ln(XR, 8, XR, scrt=FMX)
                block_ln(XR, 8, XR, scrt=FMX)
                rows_to_fm(XR, 8, FMX)

                sv0, st0 = sv.v, st.v
                with nc.Block() as block:
                    @block.vector
                    def _(v, hf=hf, sv0=sv0, st0=st0):
                        if hf == 0:
                            for c in range(8):
                                bi = v.tensor_scalar(CGT[:, c, :],
                                                     CENT[:, c, :],
                                                     GFM[:, c:c + 1], None,
                                                     op0=OP.mult)
                            sv.inc(bi)
                        else:
                            sv.inc(v.engine_nop())
                        for tc in range(2):
                            ps = ps_a if tc % 2 == 0 else ps_b
                            v.wait_ge(st.h, st0 + 2 + tc)
                            sv.inc(v.tensor_scalar(
                                SCSB[:8, 1024 * hf + 512 * tc:
                                     1024 * hf + 512 * (tc + 1)],
                                ps[:8, :], ps_m[:8, :1],
                                None, op0=OP.add))

                    @block.tensor
                    def _(t, hf=hf, sv0=sv0, st0=st0):
                        t.wait_ge(sv.h, sv0 + 1)
                        for c in range(8):
                            mm = t.matmul(ps_m[:8, :1], CENT[:, c, :],
                                          BFM[:, c:c + 1], start=(c == 0),
                                          stop=(c == 7))
                        st.inc(mm)
                        for tc in range(2):
                            ps = ps_a if tc % 2 == 0 else ps_b
                            for c in range(8):
                                mm = t.matmul(
                                    ps[:8, :], CGT[:, c, :],
                                    FMX[:, c, 512 * tc:512 * (tc + 1)],
                                    start=(c == 0), stop=(c == 7))
                            st.inc(mm)

            with nc.Block() as block:
                @block.sync
                def _(sync):
                    sv.wait(sync)
                    dS.inc(sync.dma_start(out=sc_mine.ap(),
                                          in_=SCSB[:8, :]), 16)
                    dS.wait(sync)

            with nc.Block() as block:
                @block.gpsimd
                def _(g):
                    sc.inc(g.collective_compute(
                        "AllGather", OP.bypass,
                        replica_groups=[list(range(8))],
                        ins=[sc_mine.ap()], outs=[sc_all.ap()]))
                    sc.wait(g)

            with nc.Block() as block:
                @block.sync
                def _(sync):
                    for e in range(8):
                        dS.inc(sync.dma_start(
                            out=bass.AP(TIN, 128 * e,
                                        [[1024, 128], [16, 8], [1, 16]]),
                            in_=bass.AP(sc_all, 2048 * e,
                                        [[16, 128], [16384, 8], [1, 16]])),
                            16)
                    dS.wait(sync)

                @block.gpsimd
                def _(g):
                    g.memset(TAKF[:, :], 0.0)
                    g.memset(TAKI[:, :], 0)
                    g.memset(POSM[:, :], 0.0)
                    g.memset(NEGT[:, :], NEG)
                    g.memset(ZT[:, :], 0.0)

            block_transpose(
                lambda n: TIN[:, n, :],
                8,
                lambda n: SCT[:, n, :])

            for e in range(8):
                if e < 7:
                    sv0, st0 = sv.v, st.v
                    with nc.Block() as block:
                        @block.vector
                        def _(v, e=e, sv0=sv0, st0=st0):
                            v.tensor_copy(SM[:, :], SCT[:, e, :])
                            v.copy_predicated(SM[:, :], TAKI[:, :],
                                              NEGT[:, :])
                            v.memset(LOHI[:1, 0:1], -1.0e6)
                            v.memset(LOHI[:1, 1:2], 1.0e6)
                            sv.inc(v.memset(LOHI[:1, 2:3], 0.0))
                            for k in range(BITERS):
                                v.wait_ge(st.h, st0 + 2 * k + 1)
                                sv.inc(v.tensor_scalar(
                                    GTm[:, :], SM[:, :], ps_sm[:, :1], None,
                                    op0=OP.is_ge, op1=OP.add,
                                    accum_out=CNTC[:, :1]))
                                v.wait_ge(st.h, st0 + 2 * k + 2)
                                v.tensor_scalar(SELI[:1, 0:1], ps_m[:1, :1],
                                                C2048, None, op0=OP.is_ge)
                                v.tensor_scalar(SELI[:1, 1:2], ps_m[:1, :1],
                                                C2048, None, op0=OP.is_lt)
                                v.tensor_copy(LOHI[:1, 3:4], ps_sm[:1, :1])
                                v.drain()
                                v.copy_predicated(LOHI[:1, 0:1],
                                                  SELI[:1, 0:1],
                                                  LOHI[:1, 3:4])
                                v.copy_predicated(LOHI[:1, 1:2],
                                                  SELI[:1, 1:2],
                                                  LOHI[:1, 3:4])
                                v.drain()
                                sv.inc(v.tensor_tensor(
                                    LOHI[:1, 2:3], LOHI[:1, 0:1],
                                    LOHI[:1, 1:2], op=OP.add))

                        @block.tensor
                        def _(t, e=e, sv0=sv0, st0=st0):
                            for k in range(BITERS):
                                t.wait_ge(sv.h, sv0 + 2 * k + 1)
                                st.inc(t.matmul(ps_sm[:, :1], HALFR[:1, :],
                                                LOHI[:1, 2:3], start=True,
                                                stop=True))
                                t.wait_ge(sv.h, sv0 + 2 * k + 2)
                                st.inc(t.matmul(ps_m[:1, :1], CNTC[:, :1],
                                                ONE1[:, :1], start=True,
                                                stop=True))
                            t.wait_ge(sv.h, sv0 + 2 * BITERS + 1)
                            st.inc(t.matmul(ps_sm[:, :1], ONER[:1, :],
                                            LOHI[:1, 0:1], start=True,
                                            stop=True))
                            st.inc(t.matmul(ps_s2[:, :1], ONER[:1, :],
                                            LOHI[:1, 1:2], start=True,
                                            stop=True))

                def vrank_a(v, M):
                    v.tensor_tensor_scan(ICUM[:, :], ONES[:, :], M[:, :],
                                         0.0, op0=OP.mult, op1=OP.add)
                    sv.inc(v.tensor_tensor(EXC[:, :], ICUM[:, :], M[:, :],
                                           op=OP.subtract))

                def vrank_b(v):
                    v.tensor_copy(SLBm[:8, :], ps_s2[:8, :])
                    sv.inc(v.tensor_copy(TOT8[:8, :1], ps_c[:8, 0:1]))

                def vrank_c(v):
                    sv.inc(v.tensor_scalar(SLBm[:8, :], SLBm[:8, :],
                                           ps_c[:8, 0:1], None, op0=OP.add))

                def trank_a(t):
                    t.matmul(ps_s2[:8, :], S_IND[:, :], EXC[:, :],
                             start=True, stop=True)
                    st.inc(t.matmul(ps_c[:8, 0:1], S_IND[:, :],
                                    ICUM[:, 127:128], start=True, stop=True))

                def trank_b(t):
                    st.inc(t.matmul(ps_c[:8, 0:1], L8T[:8, :8], TOT8[:8, :1],
                                    start=True, stop=True))

                def trank_c(t, M):
                    t.matmul(ps_sm[:, :], S_INDT[:8, :], SLBm[:8, :],
                             start=True, stop=False)
                    st.inc(t.matmul(ps_sm[:, :], T16S[:, :], M[:, :],
                                    start=False, stop=True))

                sv0, st0 = sv.v, st.v
                with nc.Block() as block:
                    @block.vector
                    def _(v, e=e, sv0=sv0, st0=st0):
                        if e < 7:
                            v.tensor_scalar(GTm[:, :], SM[:, :], ps_s2[:, :1],
                                            None, op0=OP.is_ge)
                            v.tensor_scalar(EQm[:, :], SM[:, :], ps_sm[:, :1],
                                            None, op0=OP.is_ge)
                            v.drain()
                            v.tensor_tensor(EQm[:, :], EQm[:, :], GTm[:, :],
                                            op=OP.subtract)
                            sv.inc(v.tensor_reduce(CNTC[:, :1], GTm[:, :],
                                                    axis=AX.X,
                                                    op=OP.add))
                            v.wait_ge(st.h, st0 + 1)
                            v.tensor_scalar(MNm[:1, :1], ps_m[:1, :1], -1.0,
                                            None, op0=OP.mult)
                            sv.inc(v.tensor_scalar(MNm[:1, :1], MNm[:1, :1],
                                                    C2048, None,
                                                    op0=OP.add))
                            vrank_a(v, EQm)
                            v.wait_ge(st.h, st0 + 3)
                            vrank_b(v)
                            v.wait_ge(st.h, st0 + 4)
                            v.drain()
                            vrank_c(v)
                            v.wait_ge(st.h, st0 + 5)
                            v.tensor_scalar(FILL[:, :], ps_sm[:, :],
                                            ps_m[:, 1:2], None, op0=OP.is_lt)
                            v.drain()
                            v.tensor_tensor(FILL[:, :], FILL[:, :], EQm[:, :],
                                            op=OP.mult)
                            v.drain()
                            sv.inc(v.tensor_tensor(SELm[:, :], GTm[:, :],
                                                    FILL[:, :],
                                                    op=OP.add))
                        else:
                            sv.inc(v.tensor_scalar(SELm[:, :], TAKF[:, :],
                                                    0.5, None,
                                                    op0=OP.is_lt))
                        svb = sv.v - sv0
                        stb = 5 if e < 7 else 0
                        vrank_a(v, SELm)
                        v.wait_ge(st.h, st0 + stb + 1)
                        vrank_b(v)
                        v.wait_ge(st.h, st0 + stb + 2)
                        v.drain()
                        vrank_c(v)
                        v.wait_ge(st.h, st0 + stb + 3)
                        v.tensor_scalar(CAPM[:, :], ps_sm[:, :], C2048, None,
                                        op0=OP.is_lt)
                        v.tensor_scalar(PVm[:, :], ps_sm[:, :],
                                        float(CAP * e), None, op0=OP.add)
                        v.drain()
                        v.tensor_tensor(SELF_[:, :], SELm[:, :], CAPM[:, :],
                                        op=OP.mult)
                        v.drain()
                        v.tensor_tensor(PVm[:, :], PVm[:, :], SELF_[:, :],
                                        op=OP.mult)
                        v.tensor_tensor(TAKF[:, :], TAKF[:, :], SELF_[:, :],
                                        op=OP.add)
                        v.drain()
                        v.tensor_tensor(POSM[:, :], POSM[:, :], PVm[:, :],
                                        op=OP.add)
                        sv.inc(v.tensor_scalar(TAKI[:, :], TAKF[:, :], 0.5,
                                                None,
                                                op0=OP.is_ge))

                    @block.tensor
                    def _(t, e=e, sv0=sv0, st0=st0):
                        if e < 7:
                            t.wait_ge(sv.h, sv0 + 1)
                            st.inc(t.matmul(ps_m[:1, :1], CNTC[:, :1],
                                            ONE1[:, :1], start=True,
                                            stop=True))
                            t.wait_ge(sv.h, sv0 + 2)
                            st.inc(t.matmul(ps_m[:, 1:2], ONER[:1, :],
                                            MNm[:1, :1], start=True,
                                            stop=True))
                            t.wait_ge(sv.h, sv0 + 3)
                            trank_a(t)
                            t.wait_ge(sv.h, sv0 + 4)
                            trank_b(t)
                            t.wait_ge(sv.h, sv0 + 5)
                            trank_c(t, EQm)
                            svb2, stb2 = 6, 5
                        else:
                            svb2, stb2 = 1, 0
                        t.wait_ge(sv.h, sv0 + svb2 + 1)
                        trank_a(t)
                        t.wait_ge(sv.h, sv0 + svb2 + 2)
                        trank_b(t)
                        t.wait_ge(sv.h, sv0 + svb2 + 3)
                        trank_c(t, SELm)

            sv0, st0 = sv.v, st.v
            with nc.Block() as block:
                @block.tensor
                def _(t, sv0=sv0, st0=st0):
                    st.inc(t.matmul(ps_s2[:16, :], SEL16[:, :], POSM[:, :],
                                    start=True, stop=True))
                    t.wait_ge(sv.h, sv0 + 1)
                    st.inc(t.transpose(ps_tr[:, :16], O3[:16, :],
                                       IDENT[:16, :16]))

                @block.vector
                def _(v, sv0=sv0, st0=st0):
                    v.wait_ge(st.h, st0 + 1)
                    sv.inc(v.tensor_copy(O3[:16, :], ps_s2[:16, :]))
                    v.wait_ge(st.h, st0 + 2)
                    v.tensor_copy(POSF[:, :], ps_tr[:, :16])
                    v.drain()
                    sv.inc(v.tensor_copy(POS32[:, :], POSF[:, :]))

            with nc.Block() as block:
                @block.sync
                def _(sync):
                    for i2 in range(4):
                        dS.inc(sync.dma_start(
                            out=bass.AP(inv_mine, 4096 * i2,
                                        [[128, 32], [1, 128]]),
                            in_=ZT[:32, :]), 16)
                    dS.wait(sync)

            with nc.Block() as block:
                @block.gpsimd
                def _(g):
                    for b in range(16):
                        dG.inc(g.indirect_dma_start(
                            bass.AP(inv_mine, 0, [[1, T], [1, 1]]),
                            IndirectOffsetOnAxis(POS32[:, b:b + 1], 0),
                            IDSF[:, b:b + 1], None), 16)
                    dG.wait(g)
                    sc.inc(g.collective_compute(
                        "AllReduce", OP.add, replica_groups=[list(range(8))],
                        ins=[inv_mine.ap()], outs=[inv_all.ap()]))
                    sc.wait(g)
                    dG.inc(g.indirect_dma_start(
                        SEGF[:, :], None,
                        bass.AP(inv_all, 0, [[16, 1024], [1, 16]]),
                        IndirectOffsetOnAxis(SEGSEL[:, :1], 0)), 16)
                    dG.wait(g)

            with nc.Block() as block:
                @block.vector
                def _(v):
                    sv.inc(v.tensor_copy(SEG32[:, :], SEGF[:, :]))

    def ffn_compute(gather_dram):
        ph = contextlib.ExitStack()
        with ph:
            XR = sbuf("FXR", [128, 8, D], stack=ph)
            FMX = sbuf("FFM", [128, 8, D], stack=ph)
            G1 = sbuf("G1", [128, 32, 256], stack=ph)
            WB = sbuf("WB", [128, 2, 8, 128], stack=ph)
            WB2 = sbuf("WB2", [128, 2, 32, 128], stack=ph)
            HFM = sbuf("HFM", [128, 8, 256], stack=ph)
            HR = sbuf("HR", [128, 2, D], stack=ph)
            B2ROW = sbuf("B2ROW", [1, 1024], stack=ph)

            with nc.Block() as block:
                @block.sync
                def _(sync):
                    dS.inc(sync.dma_start(out=GRC[:, :], in_=grep.ap()), 16)
                    dS.inc(sync.dma_start(out=BRC[:, :], in_=brep.ap()), 16)
                    dS.inc(sync.dma_start(out=B2ROW[:1, :], in_=b2row.ap()),
                           16)
                    dS.wait(sync)

            for hf in range(2):
                with nc.Block() as block:
                    @block.gpsimd
                    def _(g, hf=hf):
                        for b in range(8):
                            dG.inc(g.indirect_dma_start(
                                XR[:, b, :], None, gather_dram.ap(),
                                IndirectOffsetOnAxis(
                                    SEG32[:, 8 * hf + b:8 * hf + b + 1], 0)),
                                16)
                        dG.wait(g)

                block_ln(XR, 8, XR, scrt=FMX)
                block_ln(XR, 8, XR, gb=(GRC, BRC), scrt=FMX)
                rows_to_fm(XR, 8, FMX)

                for tc in range(4):
                    st0, ss0, sv0, dA0 = st.v, ss.v, sv.v, dA.v
                    with nc.Block() as block:
                        @block.sync
                        def _(sync, hf=hf, tc=tc, st0=st0, sv0=sv0):
                            for F in range(32):
                                if F >= 2:
                                    sync.wait_ge(st.h, st0 + F - 1)
                                dA.inc(sync.dma_start(
                                    out=WB[:, F % 2, :, :],
                                    in_=bass.AP(w1T, 128 * F,
                                                [[4096, 128],
                                                 [128 * 4096, 8],
                                                 [1, 128]])), 16)
                            for dm in range(8):
                                if dm >= 2:
                                    sync.wait_ge(st.h, st0 + 32 + dm - 1)
                                dA.inc(sync.dma_start(
                                    out=WB2[:, dm % 2, :, :],
                                    in_=bass.AP(w2T, 128 * dm,
                                                [[1024, 128],
                                                 [128 * 1024, 32],
                                                 [1, 128]])), 16)
                            sync.wait_ge(sv.h, sv0 + 16)
                            dS.inc(sync.dma_start(
                                out=bass.AP(h_mine, (8 * hf + 2 * tc) * D,
                                            [[16 * D, 128], [D, 2], [1, D]]),
                                in_=HR[:, :, :]), 16)
                            dS.wait(sync)

                        @block.tensor
                        def _(t, hf=hf, tc=tc, st0=st0, ss0=ss0, sv0=sv0,
                              dA0=dA0):
                            for F in range(32):
                                ps = ps_a if F % 2 == 0 else ps_b
                                t.wait_ge(dA.h, dA0 + 16 * (F + 1))
                                if F >= 2:
                                    t.wait_ge(ss.h, ss0 + F - 1)
                                for k in range(8):
                                    mm = t.matmul(
                                        ps[:, :256], WB[:, F % 2, k, :],
                                        FMX[:, k,
                                            256 * tc:256 * (tc + 1)],
                                        start=(k == 0), stop=(k == 7))
                                st.inc(mm)
                            for dm in range(8):
                                ps = ps_a if dm % 2 == 0 else ps_b
                                t.wait_ge(dA.h, dA0 + 512 + 16 * (dm + 1))
                                t.wait_ge(ss.h, ss0 + 32 + max(0, dm - 1))
                                for k in range(32):
                                    t.matmul(ps[:, :256],
                                             WB2[:, dm % 2, k, :],
                                             G1[:, k, :],
                                             start=(k == 0), stop=False)
                                mm = t.matmul(
                                    ps[:, :256],
                                    B2ROW[0:1, 128 * dm:128 * (dm + 1)],
                                    ONE512[:1, :256], start=False,
                                    stop=True)
                                st.inc(mm)
                            for n in range(16):
                                bb, cch = divmod(n, 8)
                                pst = ps_tr if n % 2 == 0 else ps_tr2
                                t.wait_ge(ss.h, ss0 + 33 + cch)
                                if n >= 2:
                                    t.wait_ge(sv.h, sv0 + n - 1)
                                st.inc(t.transpose(
                                    pst[:, :],
                                    HFM[:, cch, 128 * bb:128 * (bb + 1)],
                                    IDENT[:, :]))

                        @block.scalar
                        def _(s, hf=hf, tc=tc, st0=st0):
                            for F in range(32):
                                ps = ps_a if F % 2 == 0 else ps_b
                                s.wait_ge(st.h, st0 + F + 1)
                                ss.inc(s.activation(G1[:, F, :], ps[:, :256],
                                                    AF.Gelu_apprx_tanh,
                                                    bias=B1FM[:, F:F + 1]))
                            for dm in range(8):
                                ps = ps_a if dm % 2 == 0 else ps_b
                                s.wait_ge(st.h, st0 + 32 + dm + 1)
                                ss.inc(s.activation(HFM[:, dm, :],
                                                    ps[:, :256], AF.Copy))

                        @block.vector
                        def _(v, hf=hf, tc=tc, st0=st0):
                            for n in range(16):
                                bb, cch = divmod(n, 8)
                                pst = ps_tr if n % 2 == 0 else ps_tr2
                                v.wait_ge(st.h, st0 + 40 + n + 1)
                                sv.inc(v.tensor_copy(
                                    HR[:, bb, 128 * cch:128 * (cch + 1)],
                                    pst[:, :]))

        with nc.Block() as block:
            @block.gpsimd
            def _(g):
                sc.inc(g.collective_compute(
                    "AllGather", OP.bypass, replica_groups=[list(range(8))],
                    ins=[h_mine.ap()], outs=[h_all.ap()]))
                sc.wait(g)

    def owner_residual(xsrc_dram, dst_mine, dst_all):
        ph = contextlib.ExitStack()
        with ph:
            XR = sbuf("OXR", [128, 8, D], stack=ph)
            HB = sbuf("OHB", [128, 8, D], stack=ph)
            SCX = sbuf("OSCX", [128, 8, D], stack=ph)
            for hf in range(2):
                with nc.Block() as block:
                    @block.sync
                    def _(sync, hf=hf):
                        dS.inc(sync.dma_start(
                            out=XR[:, :, :],
                            in_=bass.AP(xsrc_dram, 8 * hf * D,
                                        [[16 * D, 128], [D, 8], [1, D]])),
                            16)

                    @block.gpsimd
                    def _(g, hf=hf):
                        for b in range(8):
                            dG.inc(g.indirect_dma_start(
                                HB[:, b, :], None, h_all.ap(),
                                IndirectOffsetOnAxis(
                                    POS32[:, 8 * hf + b:8 * hf + b + 1], 0)),
                                16)
                        dG.wait(g)

                block_ln(XR, 8, SCX, waits=(dS,), scrt=SCX)

                sv0 = sv.v
                with nc.Block() as block:
                    @block.vector
                    def _(v, hf=hf):
                        dG.wait(v)
                        v.tensor_tensor(SCX[:, :, :], SCX[:, :, :],
                                        HB[:, :, :], op=OP.add)
                        sv.inc(v.scalar_tensor_tensor(
                            XR[:, :, :], SCX[:, :, :], 0.5, XR[:, :, :],
                            op0=OP.mult, op1=OP.add))

                    @block.sync
                    def _(sync, hf=hf, sv0=sv0):
                        sync.wait_ge(sv.h, sv0 + 1)
                        dS.inc(sync.dma_start(
                            out=bass.AP(dst_mine, 8 * hf * D,
                                        [[16 * D, 128], [D, 8], [1, D]]),
                            in_=XR[:, :, :]), 16)
                        dS.wait(sync)

        if dst_all is not None:
            with nc.Block() as block:
                @block.gpsimd
                def _(g):
                    sc.inc(g.collective_compute(
                        "AllGather", OP.bypass,
                        replica_groups=[list(range(8))],
                        ins=[dst_mine.ap()], outs=[dst_all.ap()]))
                    sc.wait(g)

    def mamba_phase():
        ph = contextlib.ExitStack()
        with ph:
            MXC = sbuf("MXC", [128, 16, UCH], stack=ph)
            MDEL = sbuf("MDEL", [128, 16, UCH], stack=ph)
            MYA = sbuf("MYA", [128, 16, UCH], stack=ph)
            MCALL = sbuf("MCALL", [128, 16, UCH], stack=ph)
            MTRIO = sbuf("MTRIO", [128, 3 * 16 * UCH], stack=ph)
            MREG = sbuf("MREG", [128, 16 * UCH], stack=ph)
            MW = sbuf("MW", [128, 2, 8, 128], stack=ph)
            MWO = sbuf("MWO", [128, 16, 128], stack=ph)
            XPW = sbuf("XPW", [128, 16, 96], stack=ph)
            DTW = sbuf("DTW", [64, 2048], stack=ph)
            BCSEL = sbuf("BCSEL", [128, 32 * 128], stack=ph)
            XIED = sbuf("XIED", [128, UCH + 16], stack=ph)
            CACC = sbuf("CACC", [128, UCH], stack=ph)
            DBL = sbuf("DBL", [96, UCH], stack=ph)
            DUG = sbuf("DUG", [128, UCH], stack=ph)

            MT_W = 3 * 16 * UCH
            MR_W = 16 * UCH

            def trio(off, nb, w):
                if nb == 1:
                    return bass.AP(MTRIO, off, [[MT_W, 128], [1, w]])
                return bass.AP(MTRIO, off, [[MT_W, 128], [w, nb], [1, w]])

            def mreg(off, nb, w):
                if nb == 1:
                    return bass.AP(MREG, off, [[MR_W, 128], [1, w]])
                return bass.AP(MREG, off, [[MR_W, 128], [w, nb], [1, w]])

            DAH_f = trio(0, 1, NST * UCH)
            DBU_f = trio(NST * UCH, 1, NST * UCH)
            HH_f = trio(2 * NST * UCH, 1, NST * UCH)
            HH3 = trio(2 * NST * UCH, NST, UCH)

            def dah_starts():
                return bass.AP(MTRIO, 0, [[MT_W, 128], [UCH, NST]])

            def dbu_starts():
                return bass.AP(MTRIO, NST * UCH, [[MT_W, 128], [UCH, NST]])

            def hh_ends():
                return bass.AP(MTRIO, 2 * NST * UCH + UCH - 1,
                               [[MT_W, 128], [UCH, NST]])

            def hh_perm_nu():
                return bass.AP(MTRIO, 2 * NST * UCH,
                               [[MT_W, 128], [1, UCH], [UCH, NST]])

            XU = mreg(0, 2, D)
            SCRM = mreg(2 * D, 2, D)
            XNF = mreg(2 * D, 8, UCH)
            ZSTG = trio(0, 16, UCH)
            OFM = trio(NST * UCH, 8, UCH)
            YR = trio(2 * NST * UCH, 2, D)

            ss0, sv0, dS0 = ss.v, sv.v, dS.v
            with nc.Block() as block:
                @block.sync
                def _(sync):
                    dS.inc(sync.dma_start(
                        out=XPW[:, :, :],
                        in_=bass.AP(xprojT_d, 0,
                                    [[96, 128], [128 * 96, 16], [1, 96]])),
                        16)
                    dS.inc(sync.dma_start(out=DTW[:64, :], in_=dtwT_d.ap()),
                           16)
                    dS.inc(sync.dma_start(out=BCSEL[:, :], in_=bcsel_d.ap()),
                           16)
                    dS.inc(sync.dma_start(out=GRC[:, :], in_=mgrep.ap()), 16)
                    dS.inc(sync.dma_start(out=BRC[:, :], in_=mbrep.ap()), 16)

                @block.scalar
                def _(s, ss0=ss0, dS0=dS0):
                    s.wait_ge(dS.h, dS0 + 80)
                    ss.inc(s.activation(ASB[:, :], AFM[:, :], AF.Exp))

                @block.vector
                def _(v, ss0=ss0):
                    v.wait_ge(ss.h, ss0 + 1)
                    sv.inc(v.tensor_scalar(ASB[:, :], ASB[:, :], -1.0,
                                            None, op0=OP.mult))

                @block.gpsimd
                def _(g):
                    g.memset(CARRY[:, :], 0.0)
                    sg.inc(g.memset(TAILS[:, :, :], 0.0))

            for cci in range(NCH):
                with nc.Block() as block:
                    @block.gpsimd
                    def _(g, cci=cci):
                        for b in range(2):
                            dG.inc(g.indirect_dma_start(
                                XU[:, b, :], None, x1_all.ap(),
                                IndirectOffsetOnAxis(
                                    MORDER[:, 2 * cci + b:2 * cci + b + 1],
                                    0)), 16)
                        dG.wait(g)

                block_ln(XU, 2, XU, scrt=SCRM, stats=(MEAN2, VAR2, RSTD2))
                block_ln(XU, 2, XU, gb=(GRC, BRC), scrt=SCRM,
                         stats=(MEAN2, VAR2, RSTD2))
                block_transpose(
                    lambda n: XU[:, n // 8, 128 * (n % 8):128 * (n % 8 + 1)],
                    16,
                    lambda n: XNF[:, n % 8,
                                  128 * (n // 8):128 * (n // 8 + 1)])

                st0, ss0, sv0, dA0, dS0 = st.v, ss.v, sv.v, dA.v, dS.v
                with nc.Block() as block:
                    @block.sync
                    def _(sync, cci=cci, st0=st0, ss0=ss0):
                        for m in range(32):
                            if m >= 2:
                                sync.wait_ge(st.h, st0 + m - 1)
                            dA.inc(sync.dma_start(
                                out=MW[:, m % 2, :, :],
                                in_=bass.AP(inwT, 128 * m,
                                            [[2 * DI, 128],
                                             [128 * 2 * DI, 8], [1, 128]])),
                                16)
                        sync.wait_ge(ss.h, ss0 + 32)
                        dS.inc(sync.dma_start(
                            out=bass.AP(zbuf, cci * 128 * 16 * UCH,
                                        [[16 * UCH, 128], [1, 16 * UCH]]),
                            in_=ZSTG), 16)
                        dS.wait(sync)

                    @block.tensor
                    def _(t, cci=cci, st0=st0, ss0=ss0, sv0=sv0, dA0=dA0):
                        for m in range(32):
                            ps = ps_a if m % 2 == 0 else ps_b
                            t.wait_ge(dA.h, dA0 + 16 * (m + 1))
                            if m >= 2:
                                if m - 2 < 16:
                                    t.wait_ge(sv.h, sv0 + (m - 2) + 1)
                                else:
                                    t.wait_ge(ss.h, ss0 + (m - 2) + 1)
                            for k in range(8):
                                mm = t.matmul(ps[:, :UCH], MW[:, m % 2, k, :],
                                              XNF[:, k, :], start=(k == 0),
                                              stop=(k == 7))
                            st.inc(mm)

                    @block.vector
                    def _(v, cci=cci, st0=st0, ss0=ss0):
                        for m in range(16):
                            ps = ps_a if m % 2 == 0 else ps_b
                            v.wait_ge(st.h, st0 + m + 1)
                            if m >= 1:
                                v.wait_ge(ss.h, ss0 + m)
                            v.tensor_copy(XIED[:, 3:3 + UCH], ps[:, :UCH])
                            v.tensor_copy(XIED[:, 0:3], TAILS[:, m, :])
                            v.tensor_copy(TAILS[:, m, :],
                                          XIED[:, UCH:UCH + 3])
                            v.drain()
                            v.tensor_scalar(CACC[:, :], XIED[:, 0:UCH],
                                            CONVW[:, 4 * m:4 * m + 1], None,
                                            op0=OP.mult)
                            for kk in range(1, 4):
                                bi = v.scalar_tensor_tensor(
                                    CACC[:, :], XIED[:, kk:kk + UCH],
                                    CONVW[:, 4 * m + kk:4 * m + kk + 1],
                                    CACC[:, :], op0=OP.mult, op1=OP.add)
                            sv.inc(bi)

                    @block.scalar
                    def _(s, cci=cci, st0=st0, ss0=ss0, sv0=sv0):
                        for m in range(32):
                            ps = ps_a if m % 2 == 0 else ps_b
                            if m < 16:
                                s.wait_ge(sv.h, sv0 + m + 1)
                                ss.inc(s.activation(MXC[:, m, :], CACC[:, :],
                                                    AF.Silu,
                                                    bias=CONVB[:, m:m + 1]))
                            else:
                                s.wait_ge(st.h, st0 + m + 1)
                                ss.inc(s.activation(
                                    trio((m - 16) * UCH, 1, UCH),
                                    ps[:, :UCH], AF.Silu))

                st0, ss0, sv0 = st.v, ss.v, sv.v
                with nc.Block() as block:
                    @block.tensor
                    def _(t, cci=cci, st0=st0, ss0=ss0, sv0=sv0):
                        for g_ in range(16):
                            mm = t.matmul(ps_c[:96, :UCH], XPW[:, g_, :],
                                          MXC[:, g_, :], start=(g_ == 0),
                                          stop=(g_ == 15))
                        st.inc(mm)
                        for g_ in range(16):
                            ps = ps_a if g_ % 2 == 0 else ps_b
                            t.wait_ge(sv.h, sv0 + 1)
                            if g_ >= 2:
                                t.wait_ge(ss.h, ss0 + (g_ - 2) + 1)
                            st.inc(t.matmul(
                                ps[:, :UCH],
                                DTW[:64, 128 * g_:128 * (g_ + 1)],
                                DBL[:64, :], start=True, stop=True))

                    @block.vector
                    def _(v, cci=cci, st0=st0, ss0=ss0):
                        v.wait_ge(st.h, st0 + 1)
                        sv.inc(v.tensor_copy(DBL[:96, :],
                                              ps_c[:96, :UCH]))
                        for g_ in range(16):
                            v.wait_ge(ss.h, ss0 + g_ + 1)
                            sv.inc(v.tensor_tensor(MDEL[:, g_, :],
                                                    VT_[:, :], TMP1[:, :],
                                                    op=OP.add))

                    @block.scalar
                    def _(s, cci=cci, st0=st0, sv0=sv0):
                        for g_ in range(16):
                            s.wait_ge(st.h, st0 + 1 + g_ + 1)
                            if g_ >= 1:
                                s.wait_ge(sv.h, sv0 + 1 + g_)
                            ps = ps_a if g_ % 2 == 0 else ps_b
                            s.activation(VT_[:, :], ps[:, :UCH], AF.Relu,
                                         bias=DTB[:, g_:g_ + 1])
                            s.activation(TMP1[:, :], ps[:, :UCH], AF.Abs,
                                         bias=DTB[:, g_:g_ + 1])
                            s.activation(TMP1[:, :], TMP1[:, :], AF.Exp,
                                         scale=-1.0)
                            ss.inc(s.activation(TMP1[:, :], TMP1[:, :],
                                                AF.Ln, bias=1.0))

                st0, ss0 = st.v, ss.v
                with nc.Block() as block:
                    @block.tensor
                    def _(t, cci=cci, st0=st0, ss0=ss0):
                        for n in range(32):
                            pst = ps_c if n % 2 == 0 else ps_a
                            if n >= 2:
                                t.wait_ge(ss.h, ss0 + n - 1)
                            st.inc(t.matmul(
                                pst[:, :UCH],
                                BCSEL[:96, 128 * n:128 * (n + 1)],
                                DBL[:96, :], start=True, stop=True))

                    @block.scalar
                    def _(s, cci=cci, st0=st0):
                        for n in range(32):
                            pst = ps_c if n % 2 == 0 else ps_a
                            s.wait_ge(st.h, st0 + n + 1)
                            if n < 16:
                                dst = mreg(n * UCH, 1, UCH)
                            else:
                                dst = MCALL[:, n - 16, :]
                            ss.inc(s.activation(dst, pst[:, :UCH], AF.Copy))

                st0, ss0, sv0, sg0 = st.v, ss.v, sv.v, sg.v
                with nc.Block() as block:
                    @block.vector
                    def _(v, cci=cci, ss0=ss0, sv0=sv0, sg0=sg0):
                        ss.wait(v)
                        for g_ in range(16):
                            sv.inc(v.tensor_tensor(DUG[:, :],
                                                    MDEL[:, g_, :],
                                                    MXC[:, g_, :],
                                                    op=OP.mult))
                            v.wait_ge(ss.h, ss0 + 16 * (g_ + 1))
                            v.wait_ge(sg.h, sg0 + g_ + 1)
                            v.tensor_tensor(PTm[:, :], dah_starts(),
                                            CARRY[:, 16 * g_:16 * (g_ + 1)],
                                            op=OP.mult)
                            v.drain()
                            v.memset(dah_starts(), 0.0)
                            v.tensor_tensor(dbu_starts(), dbu_starts(),
                                            PTm[:, :], op=OP.add)
                            v.drain()
                            v.tensor_tensor_scan(HH_f, DAH_f, DBU_f, 0.0,
                                                 op0=OP.mult, op1=OP.add)
                            sv.inc(v.tensor_copy(
                                CARRY[:, 16 * g_:16 * (g_ + 1)],
                                hh_ends()))
                            v.tensor_tensor(HH3, HH3, MCALL[:, :, :],
                                            op=OP.mult)
                            for w in (8, 4, 2, 1):
                                v.tensor_tensor(
                                    trio(2 * NST * UCH, w, UCH),
                                    trio(2 * NST * UCH, w, UCH),
                                    trio(2 * NST * UCH + w * UCH, w, UCH),
                                    op=OP.add)
                            sv.inc(v.tensor_copy(
                                MYA[:, g_, :],
                                trio(2 * NST * UCH, 1, UCH)))

                    @block.scalar
                    def _(s, cci=cci, sv0=sv0):
                        for g_ in range(16):
                            if g_ >= 1:
                                s.wait_ge(sv.h, sv0 + 3 * g_ - 1)
                            for n in range(NST):
                                col = 16 * g_ + n
                                ss.inc(s.activation(
                                    trio(n * UCH, 1, UCH),
                                    MDEL[:, g_, :], AF.Exp,
                                    scale=ASB[:, col:col + 1]))

                    @block.gpsimd
                    def _(g, cci=cci, sv0=sv0):
                        for g_ in range(16):
                            g.wait_ge(sv.h, sv0 + 1 + 3 * g_)
                            for n in range(NST):
                                bi = g.tensor_tensor(
                                    trio(NST * UCH + n * UCH, 1, UCH),
                                    DUG[:, :], mreg(n * UCH, 1, UCH),
                                    op=OP.mult)
                            sg.inc(bi)

                st0, ss0, sv0, dA0, dS0 = st.v, ss.v, sv.v, dA.v, dS.v
                with nc.Block() as block:
                    @block.sync
                    def _(sync, cci=cci, st0=st0):
                        dS.inc(sync.dma_start(
                            out=ZSTG,
                            in_=bass.AP(zbuf, cci * 128 * 16 * UCH,
                                        [[16 * UCH, 128], [1, 16 * UCH]])),
                            16)
                        for dm in range(8):
                            if dm >= 1:
                                sync.wait_ge(st.h, st0 + dm)
                            dA.inc(sync.dma_start(
                                out=MWO[:, :, :],
                                in_=bass.AP(outwT, 128 * dm,
                                            [[D, 128], [128 * D, 16],
                                             [1, 128]])), 16)

                    @block.vector
                    def _(v, cci=cci, st0=st0, sv0=sv0, dS0=dS0):
                        v.wait_ge(dS.h, dS0 + 16)
                        for g_ in range(16):
                            v.scalar_tensor_tensor(
                                MYA[:, g_, :], MXC[:, g_, :],
                                DSKIP[:, g_:g_ + 1], MYA[:, g_, :],
                                op0=OP.mult, op1=OP.add)
                            bi = v.tensor_tensor(MYA[:, g_, :],
                                                 MYA[:, g_, :],
                                                 trio(g_ * UCH, 1, UCH),
                                                 op=OP.mult)
                        sv.inc(bi)
                        for dm in range(8):
                            ps = ps_b if dm % 2 == 0 else ps_c
                            v.wait_ge(st.h, st0 + dm + 1)
                            sv.inc(v.tensor_copy(OFM[:, dm, :], ps[:, :UCH]))

                    @block.tensor
                    def _(t, cci=cci, st0=st0, sv0=sv0, dA0=dA0):
                        t.wait_ge(sv.h, sv0 + 1)
                        for dm in range(8):
                            ps = ps_b if dm % 2 == 0 else ps_c
                            t.wait_ge(dA.h, dA0 + 16 * (dm + 1))
                            if dm >= 2:
                                t.wait_ge(sv.h, sv0 + dm)
                            for k in range(16):
                                mm = t.matmul(ps[:, :UCH], MWO[:, k, :],
                                              MYA[:, k, :], start=(k == 0),
                                              stop=(k == 15))
                            st.inc(mm)

                block_transpose(
                    lambda n: OFM[:, n % 8,
                                  128 * (n // 8):128 * (n // 8 + 1)],
                    16,
                    lambda n: bass.AP(
                        MTRIO,
                        2 * NST * UCH + (n // 8) * D + 128 * (n % 8),
                        [[MT_W, 128], [1, 128]]))

                with nc.Block() as block:
                    @block.sync
                    def _(sync, cci=cci):
                        dS.inc(sync.dma_start(
                            out=bass.AP(y_mine, (UCH * cci) * D,
                                        [[D, 128], [128 * D, 2], [1, D]]),
                            in_=YR), 16)
                        dS.wait(sync)

        with nc.Block() as block:
            @block.gpsimd
            def _(g):
                sc.inc(g.collective_compute(
                    "AllGather", OP.bypass,
                    replica_groups=[[0, 1], [2, 3], [4, 5], [6, 7]],
                    ins=[y_mine.ap()], outs=[y_pair.ap()]))
                sc.wait(g)

    def x2_phase():
        ph = contextlib.ExitStack()
        with ph:
            XR = sbuf("X2R", [128, 8, D], stack=ph)
            YB = sbuf("YB", [128, 8, D], stack=ph)
            for hf in range(2):
                with nc.Block() as block:
                    @block.sync
                    def _(sync, hf=hf):
                        dS.inc(sync.dma_start(
                            out=XR[:, :, :],
                            in_=bass.AP(x1_mine, 1024 * hf * D,
                                        [[D, 128], [128 * D, 8], [1, D]])),
                            16)

                    @block.gpsimd
                    def _(g, hf=hf):
                        for b in range(8):
                            dG.inc(g.indirect_dma_start(
                                YB[:, b, :], None, y_pair.ap(),
                                IndirectOffsetOnAxis(
                                    YSELF[:, 8 * hf + b:8 * hf + b + 1], 0)),
                                16)
                        dG.wait(g)

                    @block.vector
                    def _(v, hf=hf):
                        dS.wait(v)
                        dG.wait(v)
                        sv.inc(v.scalar_tensor_tensor(
                            XR[:, :, :], YB[:, :, :], 0.5, XR[:, :, :],
                            op0=OP.mult, op1=OP.add))

                with nc.Block() as block:
                    @block.gpsimd
                    def _(g, hf=hf):
                        for b in range(8):
                            dG.inc(g.indirect_dma_start(
                                YB[:, b, :], None, y_pair.ap(),
                                IndirectOffsetOnAxis(
                                    YSELB[:, 8 * hf + b:8 * hf + b + 1], 0)),
                                16)
                        dG.wait(g)

                    @block.vector
                    def _(v, hf=hf):
                        dG.wait(v)
                        sv.inc(v.scalar_tensor_tensor(
                            XR[:, :, :], YB[:, :, :], 0.5, XR[:, :, :],
                            op0=OP.mult, op1=OP.add))

                sv0 = sv.v
                with nc.Block() as block:
                    @block.sync
                    def _(sync, hf=hf, sv0=sv0):
                        sync.wait_ge(sv.h, sv0)
                        dS.inc(sync.dma_start(
                            out=bass.AP(x2_mine, 1024 * hf * D,
                                        [[D, 128], [128 * D, 8], [1, D]]),
                            in_=XR[:, :, :]), 16)
                        dS.wait(sync)

        with nc.Block() as block:
            @block.gpsimd
            def _(g):
                sc.inc(g.collective_compute(
                    "AllGather", OP.bypass, replica_groups=[list(range(8))],
                    ins=[x2_mine.ap()], outs=[x2_all.ap()]))
                sc.wait(g)

    with nc.Block() as block:
        @block.sync
        def _(sync):
            for src, dst in [
                (ident, IDENT), (s_ind, S_IND), (s_indT, S_INDT),
                (t16, T16S), (l8t, L8T), (onesin, ONES), (sel16, SEL16),
                (gfm, GFM),
                (bfm, BFM), (b1fm, B1FM), (idsf, IDSF),
                (segsel, SEGSEL), (morder, MORDER), (yself, YSELF),
                (yselb, YSELB), (convw, CONVW), (convb, CONVB), (dtb, DTB),
                (dskip, DSKIP), (a_fm, AFM), (consts, CONSTS),
            ]:
                dS.inc(sync.dma_start(out=dst[:, :], in_=src.ap()), 16)
            dS.inc(sync.dma_start(
                out=CENT[:, :, :],
                in_=bass.AP(centT, 0, [[8, 128], [1024, 8], [1, 8]])), 16)
            dS.wait(sync)

        @block.gpsimd
        def _(g):
            g.memset(ONE1[:, :], 1.0)
            g.memset(ONER[:1, :], 1.0)
            g.memset(HALFR[:1, :], 0.5)
            sg.inc(g.memset(ONE512[:1, :], 1.0))

    scores_and_assign(x_slab)
    ffn_compute(x_full)
    owner_residual(x_slab, x1_mine, x1_all)
    mamba_phase()
    x2_phase()
    scores_and_assign(x2_mine)
    ffn_compute(x2_all)
    owner_residual(x2_mine, out_mine, None)

    es.close()
    return nc


_CACHED = {}


def kernel(**inputs):
    xf = np.ascontiguousarray(
        np.asarray(inputs["L0_attn_x"], np.float32).reshape(T, D))
    cent = np.asarray(inputs["L0_ffn_centroids"], np.float32)
    ffng = np.asarray(inputs["L0_ffn_ln_g"], np.float32)
    ffnb = np.asarray(inputs["L0_ffn_ln_b"], np.float32)
    w1 = np.asarray(inputs["L0_ffn_w1"], np.float32)
    b1 = np.asarray(inputs["L0_ffn_b1"], np.float32)
    w2 = np.asarray(inputs["L0_ffn_w2"], np.float32)
    b2 = np.asarray(inputs["L0_ffn_b2"], np.float32)
    mgv = np.asarray(inputs["L0_attn_ln_g"], np.float32)
    mbv = np.asarray(inputs["L0_attn_ln_b"], np.float32)

    def fm(vec, nb):
        return np.ascontiguousarray(vec.reshape(nb, 128).T)

    if "nc" not in _CACHED:
        _CACHED["nc"] = build_program()
    nc = _CACHED["nc"]

    ident = np.eye(128, dtype=np.float32)
    s_ind = np.zeros((128, 8), np.float32)
    s_ind[np.arange(128), np.arange(128) // 16] = 1.0
    s_indT = np.ascontiguousarray(s_ind.T)
    t16 = np.zeros((128, 128), np.float32)
    for pp in range(128):
        for p in range(128):
            if pp // 16 == p // 16 and pp % 16 < p % 16:
                t16[pp, p] = 1.0
    onesin = np.ones((128, 128), np.float32)
    l8t_np = np.zeros((8, 8), np.float32)
    for r in range(8):
        for c in range(8):
            if r < c:
                l8t_np[r, c] = 1.0
    bcsel_np = np.zeros((128, 32 * 128), np.float32)
    for n in range(16):
        bcsel_np[64 + n, 128 * n:128 * (n + 1)] = 1.0
        bcsel_np[80 + n, 128 * (16 + n):128 * (17 + n)] = 1.0

    jj = np.arange(CAP)
    in_maps = []
    for i in range(8):
        bb_ = i >> 1
        dd = i & 1
        pre = "b_" if dd else "f_"
        inw = np.asarray(inputs[f"L0_attn_{pre}in_w"], np.float32)
        cw = np.asarray(inputs[f"L0_attn_{pre}conv_w"], np.float32)
        cb = np.asarray(inputs[f"L0_attn_{pre}conv_b"], np.float32)
        xpw = np.asarray(inputs[f"L0_attn_{pre}xproj_w"], np.float32)
        dtw = np.asarray(inputs[f"L0_attn_{pre}dt_w"], np.float32)
        dtb_ = np.asarray(inputs[f"L0_attn_{pre}dt_b"], np.float32)
        alog = np.asarray(inputs[f"L0_attn_{pre}A_log"], np.float32)
        dsk = np.asarray(inputs[f"L0_attn_{pre}D_skip"], np.float32)
        outw = np.asarray(inputs[f"L0_attn_{pre}out_w"], np.float32)

        sel16_np = np.zeros((128, 16), np.float32)
        for c in range(16):
            sel16_np[16 * i + c, c] = 1.0
        idsf_np = np.zeros((128, 16), np.float32)
        idsf_np[jj // 16, jj % 16] = (2048 * i + jj).astype(np.float32)
        segsel_np = (128 * i + np.arange(128, dtype=np.int32)).reshape(128, 1)
        morder_np = np.zeros((128, 32), np.int32)
        u = np.arange(L)
        rows = 4096 * bb_ + (4095 - u if dd else u)
        morder_np[u % 128, u // 128] = rows.astype(np.int32)
        ysf = np.zeros((128, 16), np.int32)
        ysb = np.zeros((128, 16), np.int32)
        tb = (i % 2) * 2048 + jj
        ysf[jj % 128, jj // 128] = tb.astype(np.int32)
        ysb[jj % 128, jj // 128] = (L + (4095 - tb)).astype(np.int32)

        convw_np = np.zeros((128, 64), np.float32)
        ch = np.arange(DI)
        for kk in range(4):
            convw_np[ch % 128, (ch // 128) * 4 + kk] = cw[:, kk]
        a_np = np.zeros((128, 256), np.float32)
        for n in range(16):
            a_np[ch % 128, (ch // 128) * 16 + n] = alog[:, n]

        in_maps.append({
            "x_slab": np.ascontiguousarray(xf[2048 * i:2048 * (i + 1)]),
            "x_full": xf,
            "centT": np.ascontiguousarray(cent.T),
            "gfm": fm(ffng, 8), "bfm": fm(ffnb, 8),
            "grep": np.ascontiguousarray(np.tile(ffng, (128, 1))),
            "brep": np.ascontiguousarray(np.tile(ffnb, (128, 1))),
            "w1T": np.ascontiguousarray(w1[i]),
            "b1fm": fm(b1[i], 32),
            "w2T": np.ascontiguousarray(w2[i]),
            "b2row": np.ascontiguousarray(b2[i].reshape(1, 1024)),
            "inwT": np.ascontiguousarray(inw.T),
            "convw": convw_np,
            "convb": fm(cb, 16),
            "xprojT": np.ascontiguousarray(xpw.T),
            "dtwT": np.ascontiguousarray(dtw.T),
            "dtb": fm(dtb_, 16),
            "dtbrow": np.ascontiguousarray(dtb_.reshape(1, 2048)),
            "a_fm": a_np,
            "dskip": fm(dsk, 16),
            "outwT": np.ascontiguousarray(outw.T),
            "mgrep": np.ascontiguousarray(np.tile(mgv, (128, 1))),
            "mbrep": np.ascontiguousarray(np.tile(mbv, (128, 1))),
            "ident": ident, "s_ind": s_ind, "s_indT": s_indT, "t16": t16,
            "onesin": onesin, "sel16": sel16_np, "bcsel": bcsel_np,
            "l8t": l8t_np,
            "idsf": idsf_np, "segsel": segsel_np, "morder": morder_np,
            "yself": ysf, "yselb": ysb,
            "consts": np.tile(
                np.array([[EPS, 1.0, -1.0, 0.0]], np.float32), (128, 1)),
        })

    res = run_bass_kernel_spmd(nc, in_maps, core_ids=list(range(8)))
    out = np.concatenate([np.asarray(r["out_mine"]) for r in res.results],
                         axis=0)
    return out.reshape(4, 4096, 1024)


# version 07 (final)
# speedup vs baseline: 1.0568x; 1.0288x over previous
import contextlib

import numpy as np

import concourse.bass as bass
import concourse.mybir as mybir
from concourse.bass_core_rust import IndirectOffsetOnAxis
from concourse.bass_utils import run_bass_kernel_spmd

F32 = mybir.dt.float32
I32 = mybir.dt.int32
OP = mybir.AluOpType
AF = mybir.ActivationFunctionType
AX = mybir.AxisListType

T = 16384
D = 1024
CAP = 2048
DI = 2048
NST = 16
L = 4096
UCH = 256
NCH = L // UCH
BITERS = 32
NEG = -1.0e30
EPS = 1e-5
C2048 = 2048.0
RACE_DETECT = False


class Ctr:
    def __init__(self, h):
        self.h = h
        self.v = 0

    def inc(self, bi, n=1):
        bi.then_inc(self.h, n)
        self.v += n
        return bi

    def wait(self, eng, v=None):
        eng.wait_ge(self.h, self.v if v is None else v)


def build_program():
    es = contextlib.ExitStack()
    nc = bass.Bass("TRN2", target_bir_lowering=False,
                   detect_race_conditions=RACE_DETECT)
    uid = [0]

    def sbuf(name, shape, dtype=F32, stack=None):
        uid[0] += 1
        st_ = stack if stack is not None else es
        return st_.enter_context(
            nc.sbuf_tensor(f"{name}_{uid[0]}", list(shape), dtype))

    def sem(name):
        return Ctr(es.enter_context(nc.semaphore(name)))

    def inp(name, shape, dtype=F32):
        return nc.dram_tensor(name, list(shape), dtype, kind="ExternalInput")

    x_slab = inp("x_slab", [CAP, D])
    x_full = inp("x_full", [T, D])
    centT = inp("centT", [D, 8])
    gfm = inp("gfm", [128, 8])
    bfm = inp("bfm", [128, 8])
    grep = inp("grep", [128, D])
    brep = inp("brep", [128, D])
    w1T = inp("w1T", [D, 4096])
    b1fm = inp("b1fm", [128, 32])
    w2T = inp("w2T", [4096, D])
    b2row = inp("b2row", [1, 1024])
    inwT = inp("inwT", [D, 2 * DI])
    convw = inp("convw", [128, 64])
    convb = inp("convb", [128, 16])
    xprojT_d = inp("xprojT", [DI, 96])
    dtwT_d = inp("dtwT", [64, 2048])
    dtb = inp("dtb", [128, 16])
    dtbrow = inp("dtbrow", [1, 2048])
    a_fm = inp("a_fm", [128, 256])
    dskip = inp("dskip", [128, 16])
    outwT = inp("outwT", [DI, D])
    mgrep = inp("mgrep", [128, D])
    mbrep = inp("mbrep", [128, D])
    ident = inp("ident", [128, 128])
    s_ind = inp("s_ind", [128, 8])
    s_indT = inp("s_indT", [8, 128])
    t16 = inp("t16", [128, 128])
    l8t = inp("l8t", [8, 8])
    onesin = inp("onesin", [128, 128])
    sel16 = inp("sel16", [128, 16])
    bcsel_d = inp("bcsel", [128, 32 * 128])
    idsf = inp("idsf", [128, 16])
    segsel = inp("segsel", [128, 1], I32)
    morder = inp("morder", [128, 32], I32)
    yself = inp("yself", [128, 16], I32)
    yselb = inp("yselb", [128, 16], I32)
    consts = inp("consts", [128, 4])

    out_mine = nc.dram_tensor("out_mine", [CAP, D], F32, kind="ExternalOutput")

    sc_mine = nc.dram_tensor("sc_mine", [8, CAP], F32)
    sc_all = nc.dram_tensor("sc_all", [8, 8, CAP], F32)
    inv_mine = nc.dram_tensor("inv_mine", [T, 1], F32)
    inv_all = nc.dram_tensor("inv_all", [T, 1], F32)
    h_mine = nc.dram_tensor("h_mine", [CAP, D], F32)
    h_all = nc.dram_tensor("h_all", [T, D], F32)
    x1_mine = nc.dram_tensor("x1_mine", [CAP, D], F32)
    x1_all = nc.dram_tensor("x1_all", [T, D], F32)
    x2_mine = nc.dram_tensor("x2_mine", [CAP, D], F32)
    x2_all = nc.dram_tensor("x2_all", [T, D], F32)
    y_mine = nc.dram_tensor("y_mine", [L, D], F32)
    y_pair = nc.dram_tensor("y_pair", [2 * L, D], F32)
    zbuf = nc.dram_tensor("zbuf", [NCH, 128, 16 * UCH], F32)

    MEAN = sbuf("MEAN", [128, 8])
    VAR = sbuf("VAR", [128, 8])
    RSTD = sbuf("RSTD", [128, 8])
    POSF = sbuf("POSF", [128, 16])
    POS32 = sbuf("POS32", [128, 16], I32)
    SEGF = sbuf("SEGF", [128, 16])
    SEG32 = sbuf("SEG32", [128, 16], I32)
    ONE1 = sbuf("ONE1", [128, 1])
    ONER = sbuf("ONER", [1, 128])
    HALFR = sbuf("HALFR", [1, 128])
    IDENT = sbuf("IDENT", [128, 128])
    S_IND = sbuf("S_IND", [128, 8])
    S_INDT = sbuf("S_INDT", [8, 128])
    T16S = sbuf("T16S", [128, 128])
    L8T = sbuf("L8T", [8, 8])
    ONES = sbuf("ONES", [128, 128])
    SEL16 = sbuf("SEL16", [128, 16])
    CENT = sbuf("CENT", [128, 8, 8])
    GFM = sbuf("GFM", [128, 8])
    BFM = sbuf("BFM", [128, 8])
    GRC = sbuf("GRC", [128, D])
    BRC = sbuf("BRC", [128, D])
    B1FM = sbuf("B1FM", [128, 32])
    IDSF = sbuf("IDSF", [128, 16])
    SEGSEL = sbuf("SEGSEL", [128, 1], I32)
    MORDER = sbuf("MORDER", [128, 32], I32)
    YSELF = sbuf("YSELF", [128, 16], I32)
    YSELB = sbuf("YSELB", [128, 16], I32)
    CONVW = sbuf("CONVW", [128, 64])
    CONVB = sbuf("CONVB", [128, 16])
    DTB = sbuf("DTB", [128, 16])
    ONE512 = sbuf("ONE512", [1, 512])
    DSKIP = sbuf("DSKIP", [128, 16])
    AFM = sbuf("AFM", [128, 256])
    ASB = sbuf("ASB", [128, 256])
    CARRY = sbuf("CARRY", [128, 256])
    TAILS = sbuf("TAILS", [128, 16, 3])
    PTm = sbuf("PT", [128, 16])
    VT_ = sbuf("VT", [128, UCH])
    TMP1 = sbuf("TMP1", [128, UCH])
    MEAN2 = sbuf("MEAN2", [128, 2])
    VAR2 = sbuf("VAR2", [128, 2])
    RSTD2 = sbuf("RSTD2", [128, 2])
    CONSTS = sbuf("CONSTS", [128, 4])
    nc.const_aps.register(F32, EPS, CONSTS[:, 0:1])
    nc.const_aps.register(F32, 1.0, CONSTS[:, 1:2])
    nc.const_aps.register(F32, -1.0, CONSTS[:, 2:3])
    nc.const_aps.register(F32, 0.0, CONSTS[:, 3:4])

    def psum(name, shape):
        return es.enter_context(nc.psum_tensor(name, list(shape), F32))

    ps_tr = psum("ps_tr", [128, 128])
    ps_tr2 = psum("ps_tr2", [128, 128])
    ps_a = psum("ps_a", [128, 512])
    ps_b = psum("ps_b", [128, 512])
    ps_c = psum("ps_c", [128, 512])
    ps_sm = psum("ps_sm", [128, 128])
    ps_s2 = psum("ps_s2", [128, 128])
    ps_m = psum("ps_m", [128, 8])

    dS = sem("dS")
    dA = sem("dA")
    dG = sem("dG")
    sc = sem("sc")
    sv = sem("sv")
    st = sem("st")
    ss = sem("ss")
    sg = sem("sg")

    def block_ln(X, NB, out, gb=None, waits=(), scrt=None, stats=None):
        mean, var, rstd = stats if stats else (MEAN, VAR, RSTD)
        sv0, ss0 = sv.v, ss.v
        with nc.Block() as block:
            @block.scalar
            def _(s):
                for c in waits:
                    c.wait(s)
                ss.inc(s.activation(scrt[:, :NB, :], X[:, :NB, :], AF.Square))
                s.wait_ge(sv.h, sv0 + 1)
                ss.inc(s.activation(var[:, :NB], var[:, :NB], AF.Sqrt,
                                    bias=EPS))

            @block.vector
            def _(v):
                for c in waits:
                    c.wait(v)
                v.tensor_reduce(mean[:, :NB], X[:, :NB, :], axis=AX.X,
                                op=OP.add)
                v.wait_ge(ss.h, ss0 + 1)
                v.tensor_reduce(var[:, :NB], scrt[:, :NB, :], axis=AX.X,
                                op=OP.add)
                v.drain()
                v.tensor_scalar(mean[:, :NB], mean[:, :NB], 1.0 / D, None,
                                op0=OP.mult)
                v.tensor_scalar(var[:, :NB], var[:, :NB], 1.0 / D, None,
                                op0=OP.mult)
                v.drain()
                v.tensor_tensor(rstd[:, :NB], mean[:, :NB], mean[:, :NB],
                                op=OP.mult)
                v.drain()
                sv.inc(v.tensor_tensor(var[:, :NB], var[:, :NB],
                                        rstd[:, :NB], op=OP.subtract))
                v.wait_ge(ss.h, ss0 + 2)
                v.reciprocal(rstd[:, :NB], var[:, :NB])
                v.drain()
                for b in range(NB):
                    v.tensor_scalar(out[:, b, :], X[:, b, :],
                                    mean[:, b:b + 1], None, op0=OP.subtract)
                for b in range(NB):
                    bi = v.tensor_scalar(out[:, b, :], out[:, b, :],
                                         rstd[:, b:b + 1], None, op0=OP.mult)
                if gb is not None:
                    g_r, b_r = gb
                    for b in range(NB):
                        v.tensor_tensor(out[:, b, :], out[:, b, :],
                                        g_r[:, :], op=OP.mult)
                    for b in range(NB):
                        bi = v.tensor_tensor(out[:, b, :], out[:, b, :],
                                             b_r[:, :], op=OP.add)
                sv.inc(bi)

    def block_transpose(get_in, N, put_out):
        st0, sv0 = st.v, sv.v
        with nc.Block() as block:
            @block.tensor
            def _(t):
                for n in range(N):
                    pst = ps_tr if n % 2 == 0 else ps_tr2
                    if n >= 2:
                        t.wait_ge(sv.h, sv0 + n - 1)
                    st.inc(t.transpose(pst[:, :], get_in(n), IDENT[:, :]))

            @block.vector
            def _(v):
                for n in range(N):
                    pst = ps_tr if n % 2 == 0 else ps_tr2
                    v.wait_ge(st.h, st0 + n + 1)
                    sv.inc(v.tensor_copy(put_out(n), pst[:, :]))

    def rows_to_fm(X, NB, FMt):
        block_transpose(
            lambda n: X[:, n // 8, 128 * (n % 8):128 * (n % 8 + 1)],
            NB * 8,
            lambda n: FMt[:, n % 8, 128 * (n // 8):128 * (n // 8 + 1)])

    def scores_and_assign(src_dram):
        ph = contextlib.ExitStack()
        with ph:
            XR = sbuf("XR", [128, 8, D], stack=ph)
            FMX = sbuf("FMX", [128, 8, D], stack=ph)
            SCSB = sbuf("SCSB", [8, CAP], stack=ph)
            SCT = sbuf("SCT", [128, 8, 128], stack=ph)
            TIN = sbuf("TIN", [128, 8, 128], stack=ph)
            SM = sbuf("SM", [128, 128], stack=ph)
            TAKF = sbuf("TAKF", [128, 128], stack=ph)
            TAKI = sbuf("TAKI", [128, 128], I32, stack=ph)
            POSM = sbuf("POSM", [128, 128], stack=ph)
            NEGT = sbuf("NEGT", [128, 128], stack=ph)
            GTm = sbuf("GT", [128, 128], stack=ph)
            EQm = sbuf("EQ", [128, 128], stack=ph)
            ICUM = sbuf("ICUM", [128, 128], stack=ph)
            EXC = sbuf("EXC", [128, 128], stack=ph)
            FILL = sbuf("FILL", [128, 128], stack=ph)
            SELm = sbuf("SEL", [128, 128], stack=ph)
            SELF_ = sbuf("SELF", [128, 128], stack=ph)
            CAPM = sbuf("CAPM", [128, 128], stack=ph)
            PVm = sbuf("PV", [128, 128], stack=ph)
            SLBm = sbuf("SLB", [8, 128], stack=ph)
            O3 = sbuf("O3", [16, 128], stack=ph)
            ZT = sbuf("ZT", [32, 128], stack=ph)
            CGT = sbuf("CGT", [128, 8, 8], stack=ph)
            LOHI = sbuf("LOHI", [1, 4], stack=ph)
            SELI = sbuf("SELI", [1, 2], I32, stack=ph)
            MNm = sbuf("MN", [1, 1], stack=ph)
            TOT8 = sbuf("TOT8", [8, 1], stack=ph)
            CNTC = sbuf("CNTC", [128, 1], stack=ph)

            for hf in range(2):
                with nc.Block() as block:
                    @block.sync
                    def _(sync, hf=hf):
                        dS.inc(sync.dma_start(
                            out=XR[:, :, :],
                            in_=bass.AP(src_dram, 1024 * hf * D,
                                        [[D, 128], [128 * D, 8], [1, D]])), 16)
                        dS.wait(sync)

                block_ln(XR, 8, XR, scrt=FMX)
                block_ln(XR, 8, XR, scrt=FMX)
                rows_to_fm(XR, 8, FMX)

                sv0, st0 = sv.v, st.v
                with nc.Block() as block:
                    @block.vector
                    def _(v, hf=hf, sv0=sv0, st0=st0):
                        if hf == 0:
                            for c in range(8):
                                bi = v.tensor_scalar(CGT[:, c, :],
                                                     CENT[:, c, :],
                                                     GFM[:, c:c + 1], None,
                                                     op0=OP.mult)
                            sv.inc(bi)
                        else:
                            sv.inc(v.engine_nop())
                        for tc in range(2):
                            ps = ps_a if tc % 2 == 0 else ps_b
                            v.wait_ge(st.h, st0 + 2 + tc)
                            sv.inc(v.tensor_scalar(
                                SCSB[:8, 1024 * hf + 512 * tc:
                                     1024 * hf + 512 * (tc + 1)],
                                ps[:8, :], ps_m[:8, :1],
                                None, op0=OP.add))

                    @block.tensor
                    def _(t, hf=hf, sv0=sv0, st0=st0):
                        t.wait_ge(sv.h, sv0 + 1)
                        for c in range(8):
                            mm = t.matmul(ps_m[:8, :1], CENT[:, c, :],
                                          BFM[:, c:c + 1], start=(c == 0),
                                          stop=(c == 7))
                        st.inc(mm)
                        for tc in range(2):
                            ps = ps_a if tc % 2 == 0 else ps_b
                            for c in range(8):
                                mm = t.matmul(
                                    ps[:8, :], CGT[:, c, :],
                                    FMX[:, c, 512 * tc:512 * (tc + 1)],
                                    start=(c == 0), stop=(c == 7))
                            st.inc(mm)

            with nc.Block() as block:
                @block.sync
                def _(sync):
                    sv.wait(sync)
                    dS.inc(sync.dma_start(out=sc_mine.ap(),
                                          in_=SCSB[:8, :]), 16)
                    dS.wait(sync)

            with nc.Block() as block:
                @block.gpsimd
                def _(g):
                    sc.inc(g.collective_compute(
                        "AllGather", OP.bypass,
                        replica_groups=[list(range(8))],
                        ins=[sc_mine.ap()], outs=[sc_all.ap()]))
                    sc.wait(g)

            with nc.Block() as block:
                @block.sync
                def _(sync):
                    for e in range(8):
                        dS.inc(sync.dma_start(
                            out=bass.AP(TIN, 128 * e,
                                        [[1024, 128], [16, 8], [1, 16]]),
                            in_=bass.AP(sc_all, 2048 * e,
                                        [[16, 128], [16384, 8], [1, 16]])),
                            16)
                    dS.wait(sync)

                @block.gpsimd
                def _(g):
                    g.memset(TAKF[:, :], 0.0)
                    g.memset(TAKI[:, :], 0)
                    g.memset(POSM[:, :], 0.0)
                    g.memset(NEGT[:, :], NEG)
                    g.memset(ZT[:, :], 0.0)

            block_transpose(
                lambda n: TIN[:, n, :],
                8,
                lambda n: SCT[:, n, :])

            for e in range(8):
                if e < 7:
                    sv0, st0 = sv.v, st.v
                    with nc.Block() as block:
                        @block.vector
                        def _(v, e=e, sv0=sv0, st0=st0):
                            v.tensor_copy(SM[:, :], SCT[:, e, :])
                            v.copy_predicated(SM[:, :], TAKI[:, :],
                                              NEGT[:, :])
                            v.memset(LOHI[:1, 0:1], -256.0)
                            v.memset(LOHI[:1, 1:2], 256.0)
                            sv.inc(v.memset(LOHI[:1, 2:3], 0.0))
                            for k in range(BITERS):
                                v.wait_ge(st.h, st0 + 2 * k + 1)
                                sv.inc(v.tensor_scalar(
                                    GTm[:, :], SM[:, :], ps_sm[:, :1], None,
                                    op0=OP.is_ge, op1=OP.add,
                                    accum_out=CNTC[:, :1]))
                                v.wait_ge(st.h, st0 + 2 * k + 2)
                                v.tensor_scalar(SELI[:1, 0:1], ps_m[:1, :1],
                                                C2048, None, op0=OP.is_ge)
                                v.tensor_scalar(SELI[:1, 1:2], ps_m[:1, :1],
                                                C2048, None, op0=OP.is_lt)
                                v.tensor_copy(LOHI[:1, 3:4], ps_sm[:1, :1])
                                v.drain()
                                v.copy_predicated(LOHI[:1, 0:1],
                                                  SELI[:1, 0:1],
                                                  LOHI[:1, 3:4])
                                v.copy_predicated(LOHI[:1, 1:2],
                                                  SELI[:1, 1:2],
                                                  LOHI[:1, 3:4])
                                v.drain()
                                sv.inc(v.tensor_tensor(
                                    LOHI[:1, 2:3], LOHI[:1, 0:1],
                                    LOHI[:1, 1:2], op=OP.add))

                        @block.tensor
                        def _(t, e=e, sv0=sv0, st0=st0):
                            for k in range(BITERS):
                                t.wait_ge(sv.h, sv0 + 2 * k + 1)
                                st.inc(t.matmul(ps_sm[:, :1], HALFR[:1, :],
                                                LOHI[:1, 2:3], start=True,
                                                stop=True))
                                t.wait_ge(sv.h, sv0 + 2 * k + 2)
                                st.inc(t.matmul(ps_m[:1, :1], CNTC[:, :1],
                                                ONE1[:, :1], start=True,
                                                stop=True))
                            t.wait_ge(sv.h, sv0 + 2 * BITERS + 1)
                            st.inc(t.matmul(ps_sm[:, :1], ONER[:1, :],
                                            LOHI[:1, 0:1], start=True,
                                            stop=True))
                            st.inc(t.matmul(ps_s2[:, :1], ONER[:1, :],
                                            LOHI[:1, 1:2], start=True,
                                            stop=True))

                def vrank_a(v, M):
                    v.tensor_tensor_scan(ICUM[:, :], ONES[:, :], M[:, :],
                                         0.0, op0=OP.mult, op1=OP.add)
                    sv.inc(v.tensor_tensor(EXC[:, :], ICUM[:, :], M[:, :],
                                           op=OP.subtract))

                def vrank_b(v):
                    v.tensor_copy(SLBm[:8, :], ps_s2[:8, :])
                    sv.inc(v.tensor_copy(TOT8[:8, :1], ps_c[:8, 0:1]))

                def vrank_c(v):
                    sv.inc(v.tensor_scalar(SLBm[:8, :], SLBm[:8, :],
                                           ps_c[:8, 0:1], None, op0=OP.add))

                def trank_a(t):
                    t.matmul(ps_s2[:8, :], S_IND[:, :], EXC[:, :],
                             start=True, stop=True)
                    st.inc(t.matmul(ps_c[:8, 0:1], S_IND[:, :],
                                    ICUM[:, 127:128], start=True, stop=True))

                def trank_b(t):
                    st.inc(t.matmul(ps_c[:8, 0:1], L8T[:8, :8], TOT8[:8, :1],
                                    start=True, stop=True))

                def trank_c(t, M):
                    t.matmul(ps_sm[:, :], S_INDT[:8, :], SLBm[:8, :],
                             start=True, stop=False)
                    st.inc(t.matmul(ps_sm[:, :], T16S[:, :], M[:, :],
                                    start=False, stop=True))

                sv0, st0 = sv.v, st.v
                with nc.Block() as block:
                    @block.vector
                    def _(v, e=e, sv0=sv0, st0=st0):
                        if e < 7:
                            v.tensor_scalar(GTm[:, :], SM[:, :], ps_s2[:, :1],
                                            None, op0=OP.is_ge)
                            v.tensor_scalar(EQm[:, :], SM[:, :], ps_sm[:, :1],
                                            None, op0=OP.is_ge)
                            v.drain()
                            v.tensor_tensor(EQm[:, :], EQm[:, :], GTm[:, :],
                                            op=OP.subtract)
                            sv.inc(v.tensor_reduce(CNTC[:, :1], GTm[:, :],
                                                    axis=AX.X,
                                                    op=OP.add))
                            v.wait_ge(st.h, st0 + 1)
                            v.tensor_scalar(MNm[:1, :1], ps_m[:1, :1], -1.0,
                                            None, op0=OP.mult)
                            sv.inc(v.tensor_scalar(MNm[:1, :1], MNm[:1, :1],
                                                    C2048, None,
                                                    op0=OP.add))
                            vrank_a(v, EQm)
                            v.wait_ge(st.h, st0 + 3)
                            vrank_b(v)
                            v.wait_ge(st.h, st0 + 4)
                            v.drain()
                            vrank_c(v)
                            v.wait_ge(st.h, st0 + 5)
                            v.tensor_scalar(FILL[:, :], ps_sm[:, :],
                                            ps_m[:, 1:2], None, op0=OP.is_lt)
                            v.drain()
                            v.tensor_tensor(FILL[:, :], FILL[:, :], EQm[:, :],
                                            op=OP.mult)
                            v.drain()
                            sv.inc(v.tensor_tensor(SELm[:, :], GTm[:, :],
                                                    FILL[:, :],
                                                    op=OP.add))
                        else:
                            sv.inc(v.tensor_scalar(SELm[:, :], TAKF[:, :],
                                                    0.5, None,
                                                    op0=OP.is_lt))
                        svb = sv.v - sv0
                        stb = 5 if e < 7 else 0
                        vrank_a(v, SELm)
                        v.wait_ge(st.h, st0 + stb + 1)
                        vrank_b(v)
                        v.wait_ge(st.h, st0 + stb + 2)
                        v.drain()
                        vrank_c(v)
                        v.wait_ge(st.h, st0 + stb + 3)
                        v.tensor_scalar(CAPM[:, :], ps_sm[:, :], C2048, None,
                                        op0=OP.is_lt)
                        v.tensor_scalar(PVm[:, :], ps_sm[:, :],
                                        float(CAP * e), None, op0=OP.add)
                        v.drain()
                        v.tensor_tensor(SELF_[:, :], SELm[:, :], CAPM[:, :],
                                        op=OP.mult)
                        v.drain()
                        v.tensor_tensor(PVm[:, :], PVm[:, :], SELF_[:, :],
                                        op=OP.mult)
                        v.tensor_tensor(TAKF[:, :], TAKF[:, :], SELF_[:, :],
                                        op=OP.add)
                        v.drain()
                        v.tensor_tensor(POSM[:, :], POSM[:, :], PVm[:, :],
                                        op=OP.add)
                        sv.inc(v.tensor_scalar(TAKI[:, :], TAKF[:, :], 0.5,
                                                None,
                                                op0=OP.is_ge))

                    @block.tensor
                    def _(t, e=e, sv0=sv0, st0=st0):
                        if e < 7:
                            t.wait_ge(sv.h, sv0 + 1)
                            st.inc(t.matmul(ps_m[:1, :1], CNTC[:, :1],
                                            ONE1[:, :1], start=True,
                                            stop=True))
                            t.wait_ge(sv.h, sv0 + 2)
                            st.inc(t.matmul(ps_m[:, 1:2], ONER[:1, :],
                                            MNm[:1, :1], start=True,
                                            stop=True))
                            t.wait_ge(sv.h, sv0 + 3)
                            trank_a(t)
                            t.wait_ge(sv.h, sv0 + 4)
                            trank_b(t)
                            t.wait_ge(sv.h, sv0 + 5)
                            trank_c(t, EQm)
                            svb2, stb2 = 6, 5
                        else:
                            svb2, stb2 = 1, 0
                        t.wait_ge(sv.h, sv0 + svb2 + 1)
                        trank_a(t)
                        t.wait_ge(sv.h, sv0 + svb2 + 2)
                        trank_b(t)
                        t.wait_ge(sv.h, sv0 + svb2 + 3)
                        trank_c(t, SELm)

            sv0, st0 = sv.v, st.v
            with nc.Block() as block:
                @block.tensor
                def _(t, sv0=sv0, st0=st0):
                    st.inc(t.matmul(ps_s2[:16, :], SEL16[:, :], POSM[:, :],
                                    start=True, stop=True))
                    t.wait_ge(sv.h, sv0 + 1)
                    st.inc(t.transpose(ps_tr[:, :16], O3[:16, :],
                                       IDENT[:16, :16]))

                @block.vector
                def _(v, sv0=sv0, st0=st0):
                    v.wait_ge(st.h, st0 + 1)
                    sv.inc(v.tensor_copy(O3[:16, :], ps_s2[:16, :]))
                    v.wait_ge(st.h, st0 + 2)
                    v.tensor_copy(POSF[:, :], ps_tr[:, :16])
                    v.drain()
                    sv.inc(v.tensor_copy(POS32[:, :], POSF[:, :]))

            with nc.Block() as block:
                @block.sync
                def _(sync):
                    for i2 in range(4):
                        dS.inc(sync.dma_start(
                            out=bass.AP(inv_mine, 4096 * i2,
                                        [[128, 32], [1, 128]]),
                            in_=ZT[:32, :]), 16)
                    dS.wait(sync)

            with nc.Block() as block:
                @block.gpsimd
                def _(g):
                    for b in range(16):
                        dG.inc(g.indirect_dma_start(
                            bass.AP(inv_mine, 0, [[1, T], [1, 1]]),
                            IndirectOffsetOnAxis(POS32[:, b:b + 1], 0),
                            IDSF[:, b:b + 1], None), 16)
                    dG.wait(g)
                    sc.inc(g.collective_compute(
                        "AllReduce", OP.add, replica_groups=[list(range(8))],
                        ins=[inv_mine.ap()], outs=[inv_all.ap()]))
                    sc.wait(g)
                    dG.inc(g.indirect_dma_start(
                        SEGF[:, :], None,
                        bass.AP(inv_all, 0, [[16, 1024], [1, 16]]),
                        IndirectOffsetOnAxis(SEGSEL[:, :1], 0)), 16)
                    dG.wait(g)

            with nc.Block() as block:
                @block.vector
                def _(v):
                    sv.inc(v.tensor_copy(SEG32[:, :], SEGF[:, :]))

    def ffn_compute(gather_dram):
        ph = contextlib.ExitStack()
        with ph:
            XR = sbuf("FXR", [128, 8, D], stack=ph)
            FMX = sbuf("FFM", [128, 8, D], stack=ph)
            G1 = sbuf("G1", [128, 32, 256], stack=ph)
            WB = sbuf("WB", [128, 2, 8, 128], stack=ph)
            WB2 = sbuf("WB2", [128, 2, 32, 128], stack=ph)
            HFM = sbuf("HFM", [128, 8, 256], stack=ph)
            HR = sbuf("HR", [128, 2, D], stack=ph)
            B2ROW = sbuf("B2ROW", [1, 1024], stack=ph)

            with nc.Block() as block:
                @block.sync
                def _(sync):
                    dS.inc(sync.dma_start(out=GRC[:, :], in_=grep.ap()), 16)
                    dS.inc(sync.dma_start(out=BRC[:, :], in_=brep.ap()), 16)
                    dS.inc(sync.dma_start(out=B2ROW[:1, :], in_=b2row.ap()),
                           16)
                    dS.wait(sync)

            for hf in range(2):
                with nc.Block() as block:
                    @block.gpsimd
                    def _(g, hf=hf):
                        sc.wait(g)
                        for b in range(8):
                            dG.inc(g.indirect_dma_start(
                                XR[:, b, :], None, gather_dram.ap(),
                                IndirectOffsetOnAxis(
                                    SEG32[:, 8 * hf + b:8 * hf + b + 1], 0)),
                                16)
                        dG.wait(g)

                block_ln(XR, 8, XR, scrt=FMX)
                block_ln(XR, 8, XR, gb=(GRC, BRC), scrt=FMX)
                rows_to_fm(XR, 8, FMX)

                for tc in range(4):
                    st0, ss0, sv0, dA0 = st.v, ss.v, sv.v, dA.v
                    with nc.Block() as block:
                        @block.sync
                        def _(sync, hf=hf, tc=tc, st0=st0, sv0=sv0):
                            for F in range(32):
                                if F >= 2:
                                    sync.wait_ge(st.h, st0 + F - 1)
                                dA.inc(sync.dma_start(
                                    out=WB[:, F % 2, :, :],
                                    in_=bass.AP(w1T, 128 * F,
                                                [[4096, 128],
                                                 [128 * 4096, 8],
                                                 [1, 128]])), 16)
                            for dm in range(8):
                                if dm >= 2:
                                    sync.wait_ge(st.h, st0 + 32 + dm - 1)
                                dA.inc(sync.dma_start(
                                    out=WB2[:, dm % 2, :, :],
                                    in_=bass.AP(w2T, 128 * dm,
                                                [[1024, 128],
                                                 [128 * 1024, 32],
                                                 [1, 128]])), 16)
                            sync.wait_ge(sv.h, sv0 + 16)
                            dS.inc(sync.dma_start(
                                out=bass.AP(h_mine, (8 * hf + 2 * tc) * D,
                                            [[16 * D, 128], [D, 2], [1, D]]),
                                in_=HR[:, :, :]), 16)
                            dS.wait(sync)

                        @block.tensor
                        def _(t, hf=hf, tc=tc, st0=st0, ss0=ss0, sv0=sv0,
                              dA0=dA0):
                            PS3 = (ps_a, ps_b, ps_c)
                            for F in range(32):
                                ps = PS3[F % 3]
                                t.wait_ge(dA.h, dA0 + 16 * (F + 1))
                                if F >= 3:
                                    t.wait_ge(ss.h, ss0 + F - 2)
                                for k in range(8):
                                    mm = t.matmul(
                                        ps[:, :256], WB[:, F % 2, k, :],
                                        FMX[:, k,
                                            256 * tc:256 * (tc + 1)],
                                        start=(k == 0), stop=(k == 7))
                                st.inc(mm)
                            for dm in range(8):
                                ps = PS3[(32 + dm) % 3]
                                t.wait_ge(dA.h, dA0 + 512 + 16 * (dm + 1))
                                t.wait_ge(ss.h, ss0 + 32 + max(0, dm - 2))
                                for k in range(32):
                                    t.matmul(ps[:, :256],
                                             WB2[:, dm % 2, k, :],
                                             G1[:, k, :],
                                             start=(k == 0), stop=False)
                                mm = t.matmul(
                                    ps[:, :256],
                                    B2ROW[0:1, 128 * dm:128 * (dm + 1)],
                                    ONE512[:1, :256], start=False,
                                    stop=True)
                                st.inc(mm)
                            for n in range(16):
                                bb, cch = divmod(n, 8)
                                pst = ps_tr if n % 2 == 0 else ps_tr2
                                t.wait_ge(ss.h, ss0 + 33 + cch)
                                if n >= 2:
                                    t.wait_ge(sv.h, sv0 + n - 1)
                                st.inc(t.transpose(
                                    pst[:, :],
                                    HFM[:, cch, 128 * bb:128 * (bb + 1)],
                                    IDENT[:, :]))

                        @block.scalar
                        def _(s, hf=hf, tc=tc, st0=st0):
                            PS3 = (ps_a, ps_b, ps_c)
                            for F in range(32):
                                ps = PS3[F % 3]
                                s.wait_ge(st.h, st0 + F + 1)
                                ss.inc(s.activation(G1[:, F, :], ps[:, :256],
                                                    AF.Gelu_apprx_tanh,
                                                    bias=B1FM[:, F:F + 1]))
                            for dm in range(8):
                                ps = PS3[(32 + dm) % 3]
                                s.wait_ge(st.h, st0 + 32 + dm + 1)
                                ss.inc(s.activation(HFM[:, dm, :],
                                                    ps[:, :256], AF.Copy))

                        @block.vector
                        def _(v, hf=hf, tc=tc, st0=st0):
                            for n in range(16):
                                bb, cch = divmod(n, 8)
                                pst = ps_tr if n % 2 == 0 else ps_tr2
                                v.wait_ge(st.h, st0 + 40 + n + 1)
                                sv.inc(v.tensor_copy(
                                    HR[:, bb, 128 * cch:128 * (cch + 1)],
                                    pst[:, :]))

        with nc.Block() as block:
            @block.gpsimd
            def _(g):
                sc.inc(g.collective_compute(
                    "AllGather", OP.bypass, replica_groups=[list(range(8))],
                    ins=[h_mine.ap()], outs=[h_all.ap()]))

    def owner_residual(xsrc_dram, dst_mine, dst_all):
        ph = contextlib.ExitStack()
        with ph:
            XR = sbuf("OXR", [128, 8, D], stack=ph)
            HB = sbuf("OHB", [128, 8, D], stack=ph)
            SCX = sbuf("OSCX", [128, 8, D], stack=ph)
            for hf in range(2):
                with nc.Block() as block:
                    @block.sync
                    def _(sync, hf=hf):
                        dS.inc(sync.dma_start(
                            out=XR[:, :, :],
                            in_=bass.AP(xsrc_dram, 8 * hf * D,
                                        [[16 * D, 128], [D, 8], [1, D]])),
                            16)

                    @block.gpsimd
                    def _(g, hf=hf):
                        sc.wait(g)
                        for b in range(8):
                            dG.inc(g.indirect_dma_start(
                                HB[:, b, :], None, h_all.ap(),
                                IndirectOffsetOnAxis(
                                    POS32[:, 8 * hf + b:8 * hf + b + 1], 0)),
                                16)
                        dG.wait(g)

                block_ln(XR, 8, SCX, waits=(dS,), scrt=SCX)

                sv0 = sv.v
                with nc.Block() as block:
                    @block.vector
                    def _(v, hf=hf):
                        dG.wait(v)
                        v.tensor_tensor(SCX[:, :, :], SCX[:, :, :],
                                        HB[:, :, :], op=OP.add)
                        sv.inc(v.scalar_tensor_tensor(
                            XR[:, :, :], SCX[:, :, :], 0.5, XR[:, :, :],
                            op0=OP.mult, op1=OP.add))

                    @block.sync
                    def _(sync, hf=hf, sv0=sv0):
                        sync.wait_ge(sv.h, sv0 + 1)
                        dS.inc(sync.dma_start(
                            out=bass.AP(dst_mine, 8 * hf * D,
                                        [[16 * D, 128], [D, 8], [1, D]]),
                            in_=XR[:, :, :]), 16)
                        dS.wait(sync)

        if dst_all is not None:
            with nc.Block() as block:
                @block.gpsimd
                def _(g):
                    sc.inc(g.collective_compute(
                        "AllGather", OP.bypass,
                        replica_groups=[list(range(8))],
                        ins=[dst_mine.ap()], outs=[dst_all.ap()]))

    def mamba_phase():
        ph = contextlib.ExitStack()
        with ph:
            MXC = sbuf("MXC", [128, 16, UCH], stack=ph)
            MDEL = sbuf("MDEL", [128, 16, UCH], stack=ph)
            MYA = sbuf("MYA", [128, 16, UCH], stack=ph)
            MCALL = sbuf("MCALL", [128, 16, UCH], stack=ph)
            MTRIO = sbuf("MTRIO", [128, 3 * 16 * UCH], stack=ph)
            MREG = sbuf("MREG", [128, 16 * UCH], stack=ph)
            MW = sbuf("MW", [128, 2, 8, 128], stack=ph)
            MWO = sbuf("MWO", [128, 16, 128], stack=ph)
            XPW = sbuf("XPW", [128, 16, 96], stack=ph)
            DTW = sbuf("DTW", [64, 2048], stack=ph)
            BCSEL = sbuf("BCSEL", [128, 32 * 128], stack=ph)
            XIED = sbuf("XIED", [128, UCH + 16], stack=ph)
            CACC = sbuf("CACC", [128, UCH], stack=ph)
            DBL = sbuf("DBL", [96, UCH], stack=ph)
            DUG = sbuf("DUG", [128, UCH], stack=ph)

            MT_W = 3 * 16 * UCH
            MR_W = 16 * UCH

            def trio(off, nb, w):
                if nb == 1:
                    return bass.AP(MTRIO, off, [[MT_W, 128], [1, w]])
                return bass.AP(MTRIO, off, [[MT_W, 128], [w, nb], [1, w]])

            def mreg(off, nb, w):
                if nb == 1:
                    return bass.AP(MREG, off, [[MR_W, 128], [1, w]])
                return bass.AP(MREG, off, [[MR_W, 128], [w, nb], [1, w]])

            DAH_f = trio(0, 1, NST * UCH)
            DBU_f = trio(NST * UCH, 1, NST * UCH)
            HH_f = trio(2 * NST * UCH, 1, NST * UCH)
            HH3 = trio(2 * NST * UCH, NST, UCH)

            def dah_starts():
                return bass.AP(MTRIO, 0, [[MT_W, 128], [UCH, NST]])

            def dbu_starts():
                return bass.AP(MTRIO, NST * UCH, [[MT_W, 128], [UCH, NST]])

            def hh_ends():
                return bass.AP(MTRIO, 2 * NST * UCH + UCH - 1,
                               [[MT_W, 128], [UCH, NST]])

            def hh_perm_nu():
                return bass.AP(MTRIO, 2 * NST * UCH,
                               [[MT_W, 128], [1, UCH], [UCH, NST]])

            XU = mreg(0, 2, D)
            SCRM = mreg(2 * D, 2, D)
            XNF = mreg(2 * D, 8, UCH)
            ZSTG = trio(0, 16, UCH)
            OFM = trio(NST * UCH, 8, UCH)
            YR = trio(2 * NST * UCH, 2, D)

            ss0, sv0, dS0 = ss.v, sv.v, dS.v
            with nc.Block() as block:
                @block.sync
                def _(sync):
                    dS.inc(sync.dma_start(
                        out=XPW[:, :, :],
                        in_=bass.AP(xprojT_d, 0,
                                    [[96, 128], [128 * 96, 16], [1, 96]])),
                        16)
                    dS.inc(sync.dma_start(out=DTW[:64, :], in_=dtwT_d.ap()),
                           16)
                    dS.inc(sync.dma_start(out=BCSEL[:, :], in_=bcsel_d.ap()),
                           16)
                    dS.inc(sync.dma_start(out=GRC[:, :], in_=mgrep.ap()), 16)
                    dS.inc(sync.dma_start(out=BRC[:, :], in_=mbrep.ap()), 16)

                @block.scalar
                def _(s, ss0=ss0, dS0=dS0):
                    s.wait_ge(dS.h, dS0 + 80)
                    ss.inc(s.activation(ASB[:, :], AFM[:, :], AF.Exp))

                @block.vector
                def _(v, ss0=ss0):
                    v.wait_ge(ss.h, ss0 + 1)
                    sv.inc(v.tensor_scalar(ASB[:, :], ASB[:, :], -1.0,
                                            None, op0=OP.mult))

                @block.gpsimd
                def _(g):
                    g.memset(CARRY[:, :], 0.0)
                    sg.inc(g.memset(TAILS[:, :, :], 0.0))

            for cci in range(NCH):
                with nc.Block() as block:
                    @block.gpsimd
                    def _(g, cci=cci):
                        if cci == 0:
                            sc.wait(g)
                        for b in range(2):
                            dG.inc(g.indirect_dma_start(
                                XU[:, b, :], None, x1_all.ap(),
                                IndirectOffsetOnAxis(
                                    MORDER[:, 2 * cci + b:2 * cci + b + 1],
                                    0)), 16)
                        dG.wait(g)

                block_ln(XU, 2, XU, scrt=SCRM, stats=(MEAN2, VAR2, RSTD2))
                block_ln(XU, 2, XU, gb=(GRC, BRC), scrt=SCRM,
                         stats=(MEAN2, VAR2, RSTD2))
                block_transpose(
                    lambda n: XU[:, n // 8, 128 * (n % 8):128 * (n % 8 + 1)],
                    16,
                    lambda n: XNF[:, n % 8,
                                  128 * (n // 8):128 * (n // 8 + 1)])

                st0, ss0, sv0, dA0, dS0 = st.v, ss.v, sv.v, dA.v, dS.v
                with nc.Block() as block:
                    @block.sync
                    def _(sync, cci=cci, st0=st0, ss0=ss0):
                        for m in range(32):
                            if m >= 2:
                                sync.wait_ge(st.h, st0 + m - 1)
                            dA.inc(sync.dma_start(
                                out=MW[:, m % 2, :, :],
                                in_=bass.AP(inwT, 128 * m,
                                            [[2 * DI, 128],
                                             [128 * 2 * DI, 8], [1, 128]])),
                                16)
                        sync.wait_ge(ss.h, ss0 + 32)
                        dS.inc(sync.dma_start(
                            out=bass.AP(zbuf, cci * 128 * 16 * UCH,
                                        [[16 * UCH, 128], [1, 16 * UCH]]),
                            in_=ZSTG), 16)
                        dS.wait(sync)

                    @block.tensor
                    def _(t, cci=cci, st0=st0, ss0=ss0, sv0=sv0, dA0=dA0):
                        PS3 = (ps_a, ps_b, ps_c)
                        for m in range(32):
                            ps = PS3[m % 3]
                            t.wait_ge(dA.h, dA0 + 16 * (m + 1))
                            if m >= 3:
                                if m - 3 < 16:
                                    t.wait_ge(sv.h, sv0 + (m - 3) + 1)
                                else:
                                    t.wait_ge(ss.h, ss0 + (m - 3) + 1)
                            for k in range(8):
                                mm = t.matmul(ps[:, :UCH], MW[:, m % 2, k, :],
                                              XNF[:, k, :], start=(k == 0),
                                              stop=(k == 7))
                            st.inc(mm)

                    @block.vector
                    def _(v, cci=cci, st0=st0, ss0=ss0):
                        PS3 = (ps_a, ps_b, ps_c)
                        for m in range(16):
                            ps = PS3[m % 3]
                            v.wait_ge(st.h, st0 + m + 1)
                            if m >= 1:
                                v.wait_ge(ss.h, ss0 + m)
                            v.tensor_copy(XIED[:, 3:3 + UCH], ps[:, :UCH])
                            v.tensor_copy(XIED[:, 0:3], TAILS[:, m, :])
                            v.tensor_copy(TAILS[:, m, :],
                                          XIED[:, UCH:UCH + 3])
                            v.drain()
                            v.tensor_scalar(CACC[:, :], XIED[:, 0:UCH],
                                            CONVW[:, 4 * m:4 * m + 1], None,
                                            op0=OP.mult)
                            for kk in range(1, 4):
                                bi = v.scalar_tensor_tensor(
                                    CACC[:, :], XIED[:, kk:kk + UCH],
                                    CONVW[:, 4 * m + kk:4 * m + kk + 1],
                                    CACC[:, :], op0=OP.mult, op1=OP.add)
                            sv.inc(bi)

                    @block.scalar
                    def _(s, cci=cci, st0=st0, ss0=ss0, sv0=sv0):
                        PS3 = (ps_a, ps_b, ps_c)
                        for m in range(32):
                            ps = PS3[m % 3]
                            if m < 16:
                                s.wait_ge(sv.h, sv0 + m + 1)
                                ss.inc(s.activation(MXC[:, m, :], CACC[:, :],
                                                    AF.Silu,
                                                    bias=CONVB[:, m:m + 1]))
                            else:
                                s.wait_ge(st.h, st0 + m + 1)
                                ss.inc(s.activation(
                                    trio((m - 16) * UCH, 1, UCH),
                                    ps[:, :UCH], AF.Silu))

                st0, ss0, sv0 = st.v, ss.v, sv.v
                with nc.Block() as block:
                    @block.tensor
                    def _(t, cci=cci, st0=st0, ss0=ss0, sv0=sv0):
                        for g_ in range(16):
                            mm = t.matmul(ps_c[:96, :UCH], XPW[:, g_, :],
                                          MXC[:, g_, :], start=(g_ == 0),
                                          stop=(g_ == 15))
                        st.inc(mm)
                        for g_ in range(16):
                            ps = ps_a if g_ % 2 == 0 else ps_b
                            t.wait_ge(sv.h, sv0 + 1)
                            if g_ >= 2:
                                t.wait_ge(ss.h, ss0 + (g_ - 2) + 1)
                            st.inc(t.matmul(
                                ps[:, :UCH],
                                DTW[:64, 128 * g_:128 * (g_ + 1)],
                                DBL[:64, :], start=True, stop=True))

                    @block.vector
                    def _(v, cci=cci, st0=st0, ss0=ss0):
                        v.wait_ge(st.h, st0 + 1)
                        sv.inc(v.tensor_copy(DBL[:96, :],
                                              ps_c[:96, :UCH]))
                        for g_ in range(16):
                            v.wait_ge(ss.h, ss0 + g_ + 1)
                            sv.inc(v.tensor_tensor(MDEL[:, g_, :],
                                                    VT_[:, :], TMP1[:, :],
                                                    op=OP.add))

                    @block.scalar
                    def _(s, cci=cci, st0=st0, sv0=sv0):
                        for g_ in range(16):
                            s.wait_ge(st.h, st0 + 1 + g_ + 1)
                            if g_ >= 1:
                                s.wait_ge(sv.h, sv0 + 1 + g_)
                            ps = ps_a if g_ % 2 == 0 else ps_b
                            s.activation(VT_[:, :], ps[:, :UCH], AF.Relu,
                                         bias=DTB[:, g_:g_ + 1])
                            s.activation(TMP1[:, :], ps[:, :UCH], AF.Abs,
                                         bias=DTB[:, g_:g_ + 1])
                            s.activation(TMP1[:, :], TMP1[:, :], AF.Exp,
                                         scale=-1.0)
                            ss.inc(s.activation(TMP1[:, :], TMP1[:, :],
                                                AF.Ln, bias=1.0))

                st0, ss0 = st.v, ss.v
                with nc.Block() as block:
                    @block.tensor
                    def _(t, cci=cci, st0=st0, ss0=ss0):
                        PS3 = (ps_c, ps_a, ps_b)
                        for n in range(32):
                            pst = PS3[n % 3]
                            if n >= 3:
                                t.wait_ge(ss.h, ss0 + n - 2)
                            st.inc(t.matmul(
                                pst[:, :UCH],
                                BCSEL[:96, 128 * n:128 * (n + 1)],
                                DBL[:96, :], start=True, stop=True))

                    @block.scalar
                    def _(s, cci=cci, st0=st0):
                        PS3 = (ps_c, ps_a, ps_b)
                        for n in range(32):
                            pst = PS3[n % 3]
                            s.wait_ge(st.h, st0 + n + 1)
                            if n < 16:
                                dst = mreg(n * UCH, 1, UCH)
                            else:
                                dst = MCALL[:, n - 16, :]
                            ss.inc(s.activation(dst, pst[:, :UCH], AF.Copy))

                st0, ss0, sv0, sg0 = st.v, ss.v, sv.v, sg.v
                with nc.Block() as block:
                    @block.vector
                    def _(v, cci=cci, ss0=ss0, sv0=sv0, sg0=sg0):
                        ss.wait(v)
                        for g_ in range(16):
                            sv.inc(v.tensor_tensor(DUG[:, :],
                                                    MDEL[:, g_, :],
                                                    MXC[:, g_, :],
                                                    op=OP.mult))
                            v.wait_ge(ss.h, ss0 + 16 * (g_ + 1))
                            v.wait_ge(sg.h, sg0 + g_ + 1)
                            v.tensor_tensor(PTm[:, :], dah_starts(),
                                            CARRY[:, 16 * g_:16 * (g_ + 1)],
                                            op=OP.mult)
                            v.drain()
                            v.memset(dah_starts(), 0.0)
                            v.tensor_tensor(dbu_starts(), dbu_starts(),
                                            PTm[:, :], op=OP.add)
                            v.drain()
                            v.tensor_tensor_scan(HH_f, DAH_f, DBU_f, 0.0,
                                                 op0=OP.mult, op1=OP.add)
                            sv.inc(v.tensor_copy(
                                CARRY[:, 16 * g_:16 * (g_ + 1)],
                                hh_ends()))
                            v.tensor_tensor(HH3, HH3, MCALL[:, :, :],
                                            op=OP.mult)
                            for w in (8, 4, 2, 1):
                                v.tensor_tensor(
                                    trio(2 * NST * UCH, w, UCH),
                                    trio(2 * NST * UCH, w, UCH),
                                    trio(2 * NST * UCH + w * UCH, w, UCH),
                                    op=OP.add)
                            sv.inc(v.tensor_copy(
                                MYA[:, g_, :],
                                trio(2 * NST * UCH, 1, UCH)))

                    @block.scalar
                    def _(s, cci=cci, sv0=sv0):
                        for g_ in range(16):
                            if g_ >= 1:
                                s.wait_ge(sv.h, sv0 + 3 * g_ - 1)
                            for n in range(NST):
                                col = 16 * g_ + n
                                ss.inc(s.activation(
                                    trio(n * UCH, 1, UCH),
                                    MDEL[:, g_, :], AF.Exp,
                                    scale=ASB[:, col:col + 1]))

                    @block.gpsimd
                    def _(g, cci=cci, sv0=sv0):
                        for g_ in range(16):
                            g.wait_ge(sv.h, sv0 + 1 + 3 * g_)
                            for n in range(NST):
                                bi = g.tensor_tensor(
                                    trio(NST * UCH + n * UCH, 1, UCH),
                                    DUG[:, :], mreg(n * UCH, 1, UCH),
                                    op=OP.mult)
                            sg.inc(bi)

                st0, ss0, sv0, dA0, dS0 = st.v, ss.v, sv.v, dA.v, dS.v
                with nc.Block() as block:
                    @block.sync
                    def _(sync, cci=cci, st0=st0):
                        dS.inc(sync.dma_start(
                            out=ZSTG,
                            in_=bass.AP(zbuf, cci * 128 * 16 * UCH,
                                        [[16 * UCH, 128], [1, 16 * UCH]])),
                            16)
                        for dm in range(8):
                            if dm >= 1:
                                sync.wait_ge(st.h, st0 + dm)
                            dA.inc(sync.dma_start(
                                out=MWO[:, :, :],
                                in_=bass.AP(outwT, 128 * dm,
                                            [[D, 128], [128 * D, 16],
                                             [1, 128]])), 16)

                    @block.vector
                    def _(v, cci=cci, st0=st0, sv0=sv0, dS0=dS0):
                        v.wait_ge(dS.h, dS0 + 16)
                        for g_ in range(16):
                            v.scalar_tensor_tensor(
                                MYA[:, g_, :], MXC[:, g_, :],
                                DSKIP[:, g_:g_ + 1], MYA[:, g_, :],
                                op0=OP.mult, op1=OP.add)
                            bi = v.tensor_tensor(MYA[:, g_, :],
                                                 MYA[:, g_, :],
                                                 trio(g_ * UCH, 1, UCH),
                                                 op=OP.mult)
                        sv.inc(bi)
                        PS3 = (ps_b, ps_c, ps_a)
                        for dm in range(8):
                            ps = PS3[dm % 3]
                            v.wait_ge(st.h, st0 + dm + 1)
                            sv.inc(v.tensor_copy(OFM[:, dm, :], ps[:, :UCH]))

                    @block.tensor
                    def _(t, cci=cci, st0=st0, sv0=sv0, dA0=dA0):
                        PS3 = (ps_b, ps_c, ps_a)
                        t.wait_ge(sv.h, sv0 + 1)
                        for dm in range(8):
                            ps = PS3[dm % 3]
                            t.wait_ge(dA.h, dA0 + 16 * (dm + 1))
                            if dm >= 3:
                                t.wait_ge(sv.h, sv0 + dm - 1)
                            for k in range(16):
                                mm = t.matmul(ps[:, :UCH], MWO[:, k, :],
                                              MYA[:, k, :], start=(k == 0),
                                              stop=(k == 15))
                            st.inc(mm)

                block_transpose(
                    lambda n: OFM[:, n % 8,
                                  128 * (n // 8):128 * (n // 8 + 1)],
                    16,
                    lambda n: bass.AP(
                        MTRIO,
                        2 * NST * UCH + (n // 8) * D + 128 * (n % 8),
                        [[MT_W, 128], [1, 128]]))

                with nc.Block() as block:
                    @block.sync
                    def _(sync, cci=cci):
                        dS.inc(sync.dma_start(
                            out=bass.AP(y_mine, (UCH * cci) * D,
                                        [[D, 128], [128 * D, 2], [1, D]]),
                            in_=YR), 16)
                        dS.wait(sync)

        with nc.Block() as block:
            @block.gpsimd
            def _(g):
                sc.inc(g.collective_compute(
                    "AllGather", OP.bypass,
                    replica_groups=[[0, 1], [2, 3], [4, 5], [6, 7]],
                    ins=[y_mine.ap()], outs=[y_pair.ap()]))

    def x2_phase():
        ph = contextlib.ExitStack()
        with ph:
            XR = sbuf("X2R", [128, 8, D], stack=ph)
            YB = sbuf("YB", [128, 8, D], stack=ph)
            for hf in range(2):
                with nc.Block() as block:
                    @block.sync
                    def _(sync, hf=hf):
                        dS.inc(sync.dma_start(
                            out=XR[:, :, :],
                            in_=bass.AP(x1_mine, 1024 * hf * D,
                                        [[D, 128], [128 * D, 8], [1, D]])),
                            16)

                    @block.gpsimd
                    def _(g, hf=hf):
                        sc.wait(g)
                        for b in range(8):
                            dG.inc(g.indirect_dma_start(
                                YB[:, b, :], None, y_pair.ap(),
                                IndirectOffsetOnAxis(
                                    YSELF[:, 8 * hf + b:8 * hf + b + 1], 0)),
                                16)
                        dG.wait(g)

                    @block.vector
                    def _(v, hf=hf):
                        dS.wait(v)
                        dG.wait(v)
                        sv.inc(v.scalar_tensor_tensor(
                            XR[:, :, :], YB[:, :, :], 0.5, XR[:, :, :],
                            op0=OP.mult, op1=OP.add))

                with nc.Block() as block:
                    @block.gpsimd
                    def _(g, hf=hf):
                        for b in range(8):
                            dG.inc(g.indirect_dma_start(
                                YB[:, b, :], None, y_pair.ap(),
                                IndirectOffsetOnAxis(
                                    YSELB[:, 8 * hf + b:8 * hf + b + 1], 0)),
                                16)
                        dG.wait(g)

                    @block.vector
                    def _(v, hf=hf):
                        dG.wait(v)
                        sv.inc(v.scalar_tensor_tensor(
                            XR[:, :, :], YB[:, :, :], 0.5, XR[:, :, :],
                            op0=OP.mult, op1=OP.add))

                sv0 = sv.v
                with nc.Block() as block:
                    @block.sync
                    def _(sync, hf=hf, sv0=sv0):
                        sync.wait_ge(sv.h, sv0)
                        dS.inc(sync.dma_start(
                            out=bass.AP(x2_mine, 1024 * hf * D,
                                        [[D, 128], [128 * D, 8], [1, D]]),
                            in_=XR[:, :, :]), 16)
                        dS.wait(sync)

        with nc.Block() as block:
            @block.gpsimd
            def _(g):
                sc.inc(g.collective_compute(
                    "AllGather", OP.bypass, replica_groups=[list(range(8))],
                    ins=[x2_mine.ap()], outs=[x2_all.ap()]))

    with nc.Block() as block:
        @block.sync
        def _(sync):
            for src, dst in [
                (ident, IDENT), (s_ind, S_IND), (s_indT, S_INDT),
                (t16, T16S), (l8t, L8T), (onesin, ONES), (sel16, SEL16),
                (gfm, GFM),
                (bfm, BFM), (b1fm, B1FM), (idsf, IDSF),
                (segsel, SEGSEL), (morder, MORDER), (yself, YSELF),
                (yselb, YSELB), (convw, CONVW), (convb, CONVB), (dtb, DTB),
                (dskip, DSKIP), (a_fm, AFM), (consts, CONSTS),
            ]:
                dS.inc(sync.dma_start(out=dst[:, :], in_=src.ap()), 16)
            dS.inc(sync.dma_start(
                out=CENT[:, :, :],
                in_=bass.AP(centT, 0, [[8, 128], [1024, 8], [1, 8]])), 16)
            dS.wait(sync)

        @block.gpsimd
        def _(g):
            g.memset(ONE1[:, :], 1.0)
            g.memset(ONER[:1, :], 1.0)
            g.memset(HALFR[:1, :], 0.5)
            sg.inc(g.memset(ONE512[:1, :], 1.0))

    scores_and_assign(x_slab)
    ffn_compute(x_full)
    owner_residual(x_slab, x1_mine, x1_all)
    mamba_phase()
    x2_phase()
    scores_and_assign(x2_mine)
    ffn_compute(x2_all)
    owner_residual(x2_mine, out_mine, None)

    es.close()
    return nc


_CACHED = {}


def kernel(**inputs):
    xf = np.ascontiguousarray(
        np.asarray(inputs["L0_attn_x"], np.float32).reshape(T, D))
    cent = np.asarray(inputs["L0_ffn_centroids"], np.float32)
    ffng = np.asarray(inputs["L0_ffn_ln_g"], np.float32)
    ffnb = np.asarray(inputs["L0_ffn_ln_b"], np.float32)
    w1 = np.asarray(inputs["L0_ffn_w1"], np.float32)
    b1 = np.asarray(inputs["L0_ffn_b1"], np.float32)
    w2 = np.asarray(inputs["L0_ffn_w2"], np.float32)
    b2 = np.asarray(inputs["L0_ffn_b2"], np.float32)
    mgv = np.asarray(inputs["L0_attn_ln_g"], np.float32)
    mbv = np.asarray(inputs["L0_attn_ln_b"], np.float32)

    def fm(vec, nb):
        return np.ascontiguousarray(vec.reshape(nb, 128).T)

    if "nc" not in _CACHED:
        _CACHED["nc"] = build_program()
    nc = _CACHED["nc"]

    ident = np.eye(128, dtype=np.float32)
    s_ind = np.zeros((128, 8), np.float32)
    s_ind[np.arange(128), np.arange(128) // 16] = 1.0
    s_indT = np.ascontiguousarray(s_ind.T)
    t16 = np.zeros((128, 128), np.float32)
    for pp in range(128):
        for p in range(128):
            if pp // 16 == p // 16 and pp % 16 < p % 16:
                t16[pp, p] = 1.0
    onesin = np.ones((128, 128), np.float32)
    l8t_np = np.zeros((8, 8), np.float32)
    for r in range(8):
        for c in range(8):
            if r < c:
                l8t_np[r, c] = 1.0
    bcsel_np = np.zeros((128, 32 * 128), np.float32)
    for n in range(16):
        bcsel_np[64 + n, 128 * n:128 * (n + 1)] = 1.0
        bcsel_np[80 + n, 128 * (16 + n):128 * (17 + n)] = 1.0

    jj = np.arange(CAP)
    in_maps = []
    for i in range(8):
        bb_ = i >> 1
        dd = i & 1
        pre = "b_" if dd else "f_"
        inw = np.asarray(inputs[f"L0_attn_{pre}in_w"], np.float32)
        cw = np.asarray(inputs[f"L0_attn_{pre}conv_w"], np.float32)
        cb = np.asarray(inputs[f"L0_attn_{pre}conv_b"], np.float32)
        xpw = np.asarray(inputs[f"L0_attn_{pre}xproj_w"], np.float32)
        dtw = np.asarray(inputs[f"L0_attn_{pre}dt_w"], np.float32)
        dtb_ = np.asarray(inputs[f"L0_attn_{pre}dt_b"], np.float32)
        alog = np.asarray(inputs[f"L0_attn_{pre}A_log"], np.float32)
        dsk = np.asarray(inputs[f"L0_attn_{pre}D_skip"], np.float32)
        outw = np.asarray(inputs[f"L0_attn_{pre}out_w"], np.float32)

        sel16_np = np.zeros((128, 16), np.float32)
        for c in range(16):
            sel16_np[16 * i + c, c] = 1.0
        idsf_np = np.zeros((128, 16), np.float32)
        idsf_np[jj // 16, jj % 16] = (2048 * i + jj).astype(np.float32)
        segsel_np = (128 * i + np.arange(128, dtype=np.int32)).reshape(128, 1)
        morder_np = np.zeros((128, 32), np.int32)
        u = np.arange(L)
        rows = 4096 * bb_ + (4095 - u if dd else u)
        morder_np[u % 128, u // 128] = rows.astype(np.int32)
        ysf = np.zeros((128, 16), np.int32)
        ysb = np.zeros((128, 16), np.int32)
        tb = (i % 2) * 2048 + jj
        ysf[jj % 128, jj // 128] = tb.astype(np.int32)
        ysb[jj % 128, jj // 128] = (L + (4095 - tb)).astype(np.int32)

        convw_np = np.zeros((128, 64), np.float32)
        ch = np.arange(DI)
        for kk in range(4):
            convw_np[ch % 128, (ch // 128) * 4 + kk] = cw[:, kk]
        a_np = np.zeros((128, 256), np.float32)
        for n in range(16):
            a_np[ch % 128, (ch // 128) * 16 + n] = alog[:, n]

        in_maps.append({
            "x_slab": np.ascontiguousarray(xf[2048 * i:2048 * (i + 1)]),
            "x_full": xf,
            "centT": np.ascontiguousarray(cent.T),
            "gfm": fm(ffng, 8), "bfm": fm(ffnb, 8),
            "grep": np.ascontiguousarray(np.tile(ffng, (128, 1))),
            "brep": np.ascontiguousarray(np.tile(ffnb, (128, 1))),
            "w1T": np.ascontiguousarray(w1[i]),
            "b1fm": fm(b1[i], 32),
            "w2T": np.ascontiguousarray(w2[i]),
            "b2row": np.ascontiguousarray(b2[i].reshape(1, 1024)),
            "inwT": np.ascontiguousarray(inw.T),
            "convw": convw_np,
            "convb": fm(cb, 16),
            "xprojT": np.ascontiguousarray(xpw.T),
            "dtwT": np.ascontiguousarray(dtw.T),
            "dtb": fm(dtb_, 16),
            "dtbrow": np.ascontiguousarray(dtb_.reshape(1, 2048)),
            "a_fm": a_np,
            "dskip": fm(dsk, 16),
            "outwT": np.ascontiguousarray(outw.T),
            "mgrep": np.ascontiguousarray(np.tile(mgv, (128, 1))),
            "mbrep": np.ascontiguousarray(np.tile(mbv, (128, 1))),
            "ident": ident, "s_ind": s_ind, "s_indT": s_indT, "t16": t16,
            "onesin": onesin, "sel16": sel16_np, "bcsel": bcsel_np,
            "l8t": l8t_np,
            "idsf": idsf_np, "segsel": segsel_np, "morder": morder_np,
            "yself": ysf, "yselb": ysb,
            "consts": np.tile(
                np.array([[EPS, 1.0, -1.0, 0.0]], np.float32), (128, 1)),
        })

    res = run_bass_kernel_spmd(nc, in_maps, core_ids=list(range(8)))
    out = np.concatenate([np.asarray(r["out_mine"]) for r in res.results],
                         axis=0)
    return out.reshape(4, 4096, 1024)
